# version 23
# baseline (speedup 1.0000x reference)
import numpy as np
import ml_dtypes
import concourse.bass as bass
import concourse.mybir as mybir
from concourse.bass_utils import run_bass_kernel_spmd

F32 = mybir.dt.float32
BF16 = mybir.dt.bfloat16
F8 = mybir.dt.float8e4
AF = mybir.ActivationFunctionType
ALU = mybir.AluOpType

E, H, D, M = 1024, 16, 64, 256
B, N = 2, 16384
T = 4096
TS = 512
NT = T // TS
SUB = 128
NSUB = TS // SUB
EPS = 1e-4
K0 = 20.0
LNEPS = 1e-5
DSC = float(D ** -0.25)
NDIAG = -0.5 * float(D ** -0.5)
NTOK = float(N)

ENGS = ("sync", "tensor", "vector", "scalar", "gpsimd")
SEMS = ("ld", "lg", "st", "t", "v", "s", "g", "cc")


class Plan:
    def __init__(self):
        self.ops = {e: [] for e in ENGS}
        self.cnt = {s: 0 for s in SEMS}
        self.sem = {}
        self.bankw = {}
        self.hiwater = {}

    def op(self, eng, build, waits=(), inc=None, amt=None):
        after = None
        if inc is not None:
            if amt is None:
                amt = 16 if inc in ("ld", "lg", "st") else 1
            self.cnt[inc] += amt
            after = self.cnt[inc]
        hw = self.hiwater.setdefault(eng, {})
        w = []
        for (sname, v) in waits:
            if not v:
                continue
            if inc == sname:
                v2 = min(v, self.cnt[sname] - (amt or 1))
            else:
                v2 = v
            if v2 > hw.get(sname, -1):
                hw[sname] = v2
                w.append((sname, v2))
        w = tuple(w)

        def fn(h, _b=build, _w=w, _i=inc, _a=amt):
            for (sname, val) in _w:
                h.wait_ge(self.sem[sname], val)
            ins = _b(h)
            if _i is not None:
                assert ins is not None
                ins.then_inc(self.sem[_i], _a)
        self.ops[eng].append(fn)
        return after

    def bw(self, key):
        return list(self.bankw.get(key, []))

    def bset(self, key, *marks):
        self.bankw[key] = [m for m in marks if m is not None]


def build_program():
    nc = bass.Bass("TRN2", target_bir_lowering=False)
    P = Plan()

    xT = nc.dram_tensor("xT", [E, T], F32, kind="ExternalInput").ap()
    wq_h = nc.dram_tensor("wq_h", [128, 8192], F32, kind="ExternalInput").ap()
    wk_h = nc.dram_tensor("wk_h", [128, 8192], F32, kind="ExternalInput").ap()
    wv_h = nc.dram_tensor("wv_h", [128, 8192], F32, kind="ExternalInput").ap()
    wo_h = nc.dram_tensor("wo_h", [128, 8192], F32, kind="ExternalInput").ap()
    w8_h = nc.dram_tensor("w8_h", [128, 8192], F32, kind="ExternalInput").ap()
    proj_h = nc.dram_tensor("proj_h", [128, 256], F32, kind="ExternalInput").ap()
    bcol_h = nc.dram_tensor("bcol_h", [128, 40], F32, kind="ExternalInput").ap()
    bvb_h = nc.dram_tensor("bvb_h", [128, 1024], F32, kind="ExternalInput").ap()
    sc_h = nc.dram_tensor("sc_h", [1, 8], F32, kind="ExternalInput").ap()
    sel16_h = nc.dram_tensor("sel16_h", [128, 128], BF16, kind="ExternalInput").ap()
    selh_h = nc.dram_tensor("selh_h", [128, 256], BF16, kind="ExternalInput").ap()
    ones_f_h = nc.dram_tensor("ones_f_h", [128, 128], F32, kind="ExternalInput").ap()
    ones_b_h = nc.dram_tensor("ones_b_h", [128, 128], BF16, kind="ExternalInput").ap()
    idn_f_h = nc.dram_tensor("idn_f_h", [128, 128], F32, kind="ExternalInput").ap()
    ck_h = nc.dram_tensor("ck_h", [128, 4], F32, kind="ExternalInput").ap()
    maskt_h = nc.dram_tensor("maskt_h", [16, 1040], BF16, kind="ExternalInput").ap()

    xoT = nc.dram_tensor("xoT", [E, T], F32, kind="ExternalOutput").ap()
    x1_hbm = nc.dram_tensor("x1_hbm", [E, T], F32, kind="ExternalOutput").ap()
    q_hbm = nc.dram_tensor("q_hbm", [E, T], BF16, kind="ExternalOutput").ap()
    dbg = nc.dram_tensor("dbg", [128, 4112], F32, kind="ExternalOutput").ap()

    p_in_a = nc.dram_tensor("p_in_a", [65, 2056], F32)
    p_in_b = nc.dram_tensor("p_in_b", [65, 2056], F32)
    p_out_a = nc.dram_tensor("p_out_a", [65, 2056], F32)
    p_out_b = nc.dram_tensor("p_out_b", [65, 2056], F32)
    stab_in = nc.dram_tensor("stab_in", [1, 128], F32)
    stab_out = nc.dram_tensor("stab_out", [1, 128], F32)

    import contextlib
    stack = contextlib.ExitStack()
    sb = lambda name, shape, dt: stack.enter_context(nc.sbuf_tensor(name, shape, dt))
    ps = lambda name, shape: stack.enter_context(nc.psum_tensor(name, shape, F32))

    wq = sb("wq", [128, 8192], BF16)
    wk = sb("wk", [128, 8192], BF16)
    wv = sb("wv", [128, 8192], BF16)
    wo = sb("wo", [128, 8192], BF16)
    w8 = sb("w8", [128, 8192], F8)
    proj2b = sb("proj2b", [128, 256], BF16)
    sel16 = sb("sel16", [128, 128], BF16)
    selh = sb("selh", [128, 256], BF16)
    ones_f = sb("ones_f", [128, 128], F32)
    ones_b = sb("ones_b", [128, 128], BF16)
    idn_f = sb("idn_f", [128, 128], F32)
    ck = sb("ck", [128, 4], F32)
    bcol = sb("bcol", [128, 40], F32)
    hbf = sb("hbf", [128, 8], F32)
    bvb = sb("bvb", [128, 1024], BF16)
    scrow = sb("scrow", [1, 32], F32)
    smallcol = sb("smallcol", [128, 16], F32)
    rows = sb("rows", [1, 2048], F32)
    rows_b = sb("rows_b", [1, 512], BF16)
    bufA = sb("bufA", [128, 4096], F32)
    bufB = sb("bufB", [128, 4096], F32)
    bufD = sb("bufD", [128, 4096], BF16)
    kt = sb("kt", [128, 4096], BF16)
    q8t = sb("q8t", [128, 4096], F8)
    t1sc = sb("t1sc", [128, 1024], F32)
    sqch = sb("sqch", [128, 1024], BF16)
    v1t = sb("v1t", [128, 4160], BF16)
    ekt = sb("ekt", [128, 4112], BF16)
    cacc = sb("cacc", [128, 4112], F32)
    dgrows = sb("dgrows", [16, 512], F32)
    dcols = sb("dcols", [128, 32], F32)
    lnb = sb("lnb", [128, 32], F32)
    smax16 = sb("smax16", [128, 16], F32)
    rcb = sb("rcb", [128, 1024], BF16)
    rmcols = sb("rmcols", [128, 16], F32)
    rmT = sb("rmT", [16, 512], F32)
    t16 = sb("t16", [16, 512], F32)
    eps16 = sb("eps16", [16, 512], BF16)
    ctxden = sb("ctxden", [128, 2080], BF16)
    csxt = sb("csxt", [16, 1040], BF16)
    maskt = sb("maskt", [16, 1040], BF16)
    eqt = sb("eqt", [128, 2048], BF16)
    attnst = sb("attnst", [64, 4096], BF16)
    stg = sb("stg", [1, 64], F32)
    vcol = sb("vcol", [128, 4], F32)

    ps_a = ps("ps_a", [128, 512])
    ps_b = ps("ps_b", [128, 512])
    ps_c = ps("ps_c", [128, 512])
    ps_d = ps("ps_d", [128, 512])
    ps_e = ps("ps_e", [128, 512])
    ps_f = ps("ps_f", [128, 512])
    ps_g = ps("ps_g", [128, 512])
    ps_h = ps("ps_h", [128, 512])
    BK = {id(b): nm for b, nm in [(ps_a, "a"), (ps_b, "b"), (ps_c, "c"), (ps_d, "d"),
                                  (ps_e, "e"), (ps_f, "f"), (ps_g, "g"), (ps_h, "h")]}

    def wchunk(wt, oc, kc, width=128):
        base = (oc * 8 + kc) * 128
        return wt[:, base:base + width]

    def wv_moving(kc, half):
        v = wv[:].rearrange("p (o k c) -> p o k c", o=8, k=8)
        return v[:, half * 4:(half + 1) * 4, kc, :]

    P.op("sync", lambda s: s.dma_start(out=ones_f[:], in_=ones_f_h[:, :]), inc="ld")
    P.op("sync", lambda s: s.dma_start(out=ones_b[:], in_=ones_b_h[:, :]), inc="ld")
    P.op("sync", lambda s: s.dma_start(out=idn_f[:], in_=idn_f_h[:, :]), inc="ld")
    P.op("sync", lambda s: s.dma_start(out=ck[:], in_=ck_h[:, :]), inc="ld")
    P.op("sync", lambda s: s.dma_start(out=maskt[:], in_=maskt_h[:, :]), inc="ld")
    P.op("sync", lambda s: s.dma_start(out=sel16[:], in_=sel16_h[:, :]), inc="ld")
    P.op("sync", lambda s: s.dma_start(out=selh[:], in_=selh_h[:, :]), inc="ld")
    P.op("sync", lambda s: s.dma_start(out=bcol[:], in_=bcol_h[:, :]), inc="ld")
    P.op("sync", lambda s: s.dma_start(out=scrow[:, 0:8], in_=sc_h[:, :]), inc="ld")
    bvbm = P.op("gpsimd", lambda g: g.dma_start(out=bvb[:], in_=bvb_h[:, :]), inc="lg")
    wqm = P.op("gpsimd", lambda g: g.dma_start(out=wq[:], in_=wq_h[:, :]), inc="lg")
    wkm = P.op("gpsimd", lambda g: g.dma_start(out=wk[:], in_=wk_h[:, :]), inc="lg")
    wvm = P.op("gpsimd", lambda g: g.dma_start(out=wv[:], in_=wv_h[:, :]), inc="lg")
    wom = P.op("gpsimd", lambda g: g.dma_start(out=wo[:], in_=wo_h[:, :]), inc="lg")
    w8ld = []
    for half in range(2):
        w = P.op("gpsimd", lambda g, _h=half: g.dma_start(
            out=bufA[:], in_=w8_h[:, _h * 4096:(_h + 1) * 4096]),
            waits=[("v", w8ld[0])] if half == 1 else (), inc="lg")
        c = P.op("vector", lambda v, _h=half: v.tensor_copy(
            w8[:, _h * 4096:(_h + 1) * 4096], bufA[:]), waits=[("lg", w)], inc="v")
        w8ld.append(c)
    wp = P.op("sync", lambda s: s.dma_start(out=t1sc[:, 0:256], in_=proj_h[:, :]), inc="ld")
    proj_ready = P.op("scalar", lambda h: h.mul(proj2b[:], t1sc[:, 0:256], DSC),
                      waits=[("ld", wp)], inc="s")
    consts_ld = P.cnt["ld"]

    def _preload(h):
        from concourse.hw_specs import get_activation_tables
        tabs = get_activation_tables(nc.m.arch)
        sid = None
        for idx, funcs in enumerate(tabs.values()):
            if AF.Exp in funcs and AF.Ln in funcs:
                sid = idx
                break
        assert sid is not None
        return h.add_instruction(mybir.InstLoadActFuncSet(
            name=nc.get_next_instruction_name(), act_func_set_id=sid, ins=[], outs=[]))
    P.op("scalar", _preload)

    sc_ok = P.op("vector", lambda v: v.reciprocal(scrow[:, 16:24], scrow[:, 0:8]),
                 waits=[("ld", consts_ld)], inc="v")
    sc_ok = P.op("vector", lambda v: v.tensor_scalar(
        out=scrow[:, 8:16], in0=scrow[:, 0:8], scalar1=scrow[0:1, 1:2], scalar2=0.5,
        op0=ALU.mult, op1=ALU.mult), inc="v")
    t_b = P.op("tensor", lambda t: t.matmul(ps_a[:, 0:16], ones_f[0:1, 0:128],
                                            scrow[0:1, 0:16], start=True, stop=True),
               waits=[("v", sc_ok)], inc="t")
    sc_col = P.op("vector", lambda v: v.tensor_copy(smallcol[:, 0:16], ps_a[:, 0:16]),
                  waits=[("t", t_b)], inc="v")
    hbf_ok = P.op("vector", lambda v: v.tensor_scalar_mul(hbf[:], bcol[:, 32:40], 0.5),
                  waits=[("ld", consts_ld)], inc="v")
    P.op("vector", lambda v: v.memset(cacc[:], 0.0), inc="v")
    P.op("vector", lambda v: v.memset(smax16[:], -1e30), inc="v")
    P.op("vector", lambda v: v.memset(
        v1t[:].rearrange("p (s h c) -> p s h c", s=4, h=16)[:, :, :, 64:65], 1.0), inc="v")
    P.op("vector", lambda v: v.memset(
        ekt[:].rearrange("p (h c) -> p h c", h=16)[:, :, 256:257], 1.0), inc="v")
    setup_v = P.cnt["v"]
    P.bset("bank_a", ("v", sc_col))

    def ln_stats(buf, want_rinvs, extra_waits=(), src_bf16=False):
        ew = list(extra_waits)
        t_sx = None
        mmq = {}
        if src_bf16:
            def sxmm(t):
                last = None
                for e in range(8):
                    last = t.matmul(ps_a[0:1, :], ones_b[:, 0:1], buf[:, e * 512:(e + 1) * 512],
                                    start=(e == 0), stop=(e == 7))
                return last
            t_sx = P.op("tensor", sxmm,
                        waits=P.bw("bank_a") + [("v", P.cnt["v"]), ("s", P.cnt["s"])] + ew, inc="t")
        else:
            for e in range(8):
                hf = e % 4
                cm = P.op("vector", lambda v, _e=e, _hf=hf: v.tensor_copy(
                    eqt[:, _hf * 512:(_hf + 1) * 512], buf[:, _e * 512:(_e + 1) * 512]),
                    waits=([("t", mmq[e - 4])] if e >= 4 else P.bw("eqt")) + ew, inc="v")
                t_sx = P.op("tensor", lambda t, _e=e, _hf=hf: t.matmul(
                    ps_a[0:1, :], ones_b[:, 0:1], eqt[:, _hf * 512:(_hf + 1) * 512],
                    start=(_e == 0), stop=(_e == 7)),
                    waits=[("v", cm)] + (P.bw("bank_a") if e == 0 else []), inc="t")
                mmq[e] = t_sx
        t_sq = None
        for e in range(8):
            hf = e % 2
            s_sq = P.op("scalar", lambda h, _e=e, _hf=hf: h.activation(
                out=sqch[:, _hf * 512:(_hf + 1) * 512], in_=buf[:, _e * 512:(_e + 1) * 512],
                func=AF.Square, bias=0.0, scale=1.0),
                waits=([("t", mmq.get(100 + e - 2, P.cnt["t"]))] if e >= 2 else P.bw("sqch")) + ew,
                inc="s")
            t_sq = P.op("tensor", lambda t, _e=e, _hf=hf: t.matmul(
                ps_b[0:1, :], ones_b[:, 0:1], sqch[:, _hf * 512:(_hf + 1) * 512],
                start=(_e == 0), stop=(_e == 7)),
                waits=[("s", s_sq)] + (P.bw("bank_b") if e == 0 else []), inc="t")
            mmq[100 + e] = t_sq
        if not src_bf16:
            P.bset("eqt", ("t", P.cnt["t"]))
        P.bset("sqch", ("t", P.cnt["t"]))
        v1 = P.op("vector", lambda v: v.tensor_scalar_mul(rows[:, 0:512], ps_a[0:1, :], 1.0 / E),
                  waits=[("t", t_sx)], inc="v")
        v1b = P.op("vector", lambda v: v.tensor_scalar_mul(rows_b[:, 0:512], ps_a[0:1, :], 1.0 / E),
                   inc="v")
        P.bset("bank_a", ("v", v1b))
        v2 = P.op("vector", lambda v: v.tensor_scalar_mul(rows[:, 512:1024], ps_b[0:1, :], 1.0 / E),
                  waits=[("t", t_sq)], inc="v")
        P.bset("bank_b", ("v", v2))
        v3 = P.op("vector", lambda v: v.tensor_mul(rows[:, 1024:1536], rows[:, 0:512], rows[:, 0:512]),
                  inc="v")
        v4 = P.op("vector", lambda v: v.scalar_tensor_tensor(
            out=rows[:, 1024:1536], in0=rows[:, 512:1024], scalar=LNEPS,
            in1=rows[:, 1024:1536], op0=ALU.add, op1=ALU.subtract), inc="v")
        s1 = P.op("scalar", lambda h: h.activation(out=rows[:, 512:1024], in_=rows[:, 1024:1536],
                                                   func=AF.Ln, bias=0.0, scale=1.0),
                  waits=[("v", v4)], inc="s")
        s2 = P.op("scalar", lambda h: h.activation(out=rows[:, 1536:2048], in_=rows[:, 512:1024],
                                                   func=AF.Exp, bias=0.0, scale=-0.5),
                  inc="s")
        if want_rinvs:
            vr = P.op("vector", lambda v: v.tensor_scalar_mul(
                rows[:, 1024:1536], rows[:, 1536:2048], scrow[0:1, 16:17]),
                waits=[("s", s2)], inc="v")
            return vr, s2
        return None, s2

    def bcast_b(row_ap, bank, bankkey, n=512, wait_marks=()):
        return P.op("tensor", lambda t: t.matmul(bank[:, 0:n], ones_b[0:1, 0:128], row_ap,
                                                 start=True, stop=True),
                    waits=list(wait_marks) + P.bw(bankkey), inc="t")

    def bcast_f(row_ap, bank, bankkey, n=512, wait_marks=()):
        return P.op("tensor", lambda t: t.matmul(bank[:, 0:n], ones_f[0:1, 0:128], row_ap,
                                                 start=True, stop=True),
                    waits=list(wait_marks) + P.bw(bankkey), inc="t")

    st_x1 = {}
    st_q = {}
    prev_x1_store = None
    for it in range(NT):
        lx = P.op("sync", lambda s, _it=it: s.dma_start(
            out=bufA[:].rearrange("p (e t) -> p e t", e=8),
            in_=xT.rearrange("(e p) t -> p e t", p=128)[:, :, _it * TS:(_it + 1) * TS]),
            waits=([("v", setup_v)] if it == 0 else []) + P.bw("bufAx"), inc="ld")
        vr, _ = ln_stats(bufA, want_rinvs=True, extra_waits=[("ld", lx)])
        bm = bcast_b(rows_b[0:1, 0:512], ps_b, "bank_b", wait_marks=[("v", P.cnt["v"])])
        bm_ = bm
        br = bcast_f(rows[0:1, 1024:1536], ps_a, "bank_a", wait_marks=[("v", vr)])
        qm = None
        qmark = {}
        for e in range(8):
            hf = e % 2
            P.op("vector", lambda v, _e=e, _hf=hf: v.scalar_tensor_tensor(
                out=t1sc[:, _hf * 512:(_hf + 1) * 512], in0=bufA[:, _e * 512:(_e + 1) * 512],
                scalar=1.0, in1=ps_b[:, :], op0=ALU.mult, op1=ALU.subtract),
                waits=[("t", bm)] + P.bw(f"t1sc{hf}"), inc="v")
            qm = P.op("vector", lambda v, _e=e, _hf=hf: v.tensor_mul(
                q8t[:, _e * 512:(_e + 1) * 512], t1sc[:, _hf * 512:(_hf + 1) * 512], ps_a[:, :]),
                waits=[("t", br)], inc="v")
            P.bset(f"t1sc{hf}", ("v", qm))
            qmark[e] = qm
        quant_done = qm
        P.bset("bank_a", ("v", quant_done))
        P.bset("bank_b", ("v", quant_done))
        for oc in range(8):
            bank = (ps_c, ps_d, ps_e, ps_f)[oc % 4]
            bkey = "bank_" + BK[id(bank)]
            tm = None
            for kp in range(4):
                tm = P.op("tensor", lambda t, _oc=oc, _kp=kp, _bank=bank: t.matmul(
                    _bank[:, :],
                    wchunk(w8, _oc, 2 * _kp, width=256).rearrange("p (k c) -> p k c", k=2),
                    q8t[:, _kp * 1024:(_kp + 1) * 1024].rearrange("p (k n) -> p k n", k=2),
                    start=(_kp == 0), stop=(_kp == 3),
                    perf_mode=mybir.MatmulPerfMode.DoubleRow),
                    waits=([("v", qmark[2 * kp + 1])] if oc == 0 else []) +
                          (P.bw(bkey) if kp == 0 else []),
                    inc="t" if kp == 3 else None)
            sm = P.op("scalar", lambda h, _oc=oc, _bank=bank: h.activation(
                out=t1sc[:, (_oc % 2) * 512:(_oc % 2 + 1) * 512], in_=_bank[:, :], func=AF.Identity,
                bias=hbf[:, _oc:_oc + 1], scale=smallcol[:, 8:9]),
                waits=[("t", tm)] + P.bw(f"t1sc{oc % 2}"), inc="s")
            P.bset(bkey, ("s", sm))
            va = P.op("vector", lambda v, _oc=oc: v.tensor_add(
                bufB[:, _oc * 512:(_oc + 1) * 512], bufA[:, _oc * 512:(_oc + 1) * 512],
                t1sc[:, (_oc % 2) * 512:(_oc % 2 + 1) * 512]),
                waits=[("s", sm)] + ([("st", prev_x1_store)] if (oc == 0 and prev_x1_store) else []),
                inc="v")
            P.bset(f"t1sc{oc % 2}", ("v", va))
        x1_done = P.cnt["v"]
        P.bset("bufAx", ("v", x1_done), ("t", P.cnt["t"]), ("s", P.cnt["s"]))
        st1 = P.op("sync", lambda s, _it=it: s.dma_start(
            out=x1_hbm.rearrange("(e p) t -> p e t", p=128)[:, :, _it * TS:(_it + 1) * TS],
            in_=bufB[:].rearrange("p (e t) -> p e t", e=8)),
            waits=[("v", x1_done)], inc="st")
        st_x1[it] = st1
        prev_x1_store = st1
        vr2, s_r2 = ln_stats(bufB, want_rinvs=False)
        bm2 = bcast_b(rows_b[0:1, 0:512], ps_b, "bank_b", wait_marks=[("v", P.cnt["v"])])
        br2 = bcast_f(rows[0:1, 1536:2048], ps_a, "bank_a", wait_marks=[("s", s_r2)])
        ym = None
        ymark = {}
        for e in range(8):
            hf = e % 2
            P.op("vector", lambda v, _e=e, _hf=hf: v.scalar_tensor_tensor(
                out=t1sc[:, _hf * 512:(_hf + 1) * 512], in0=bufB[:, _e * 512:(_e + 1) * 512],
                scalar=1.0, in1=ps_b[:, :], op0=ALU.mult, op1=ALU.subtract),
                waits=[("t", bm2)] + P.bw(f"t1sc{hf}"), inc="v")
            ym = P.op("vector", lambda v, _e=e, _hf=hf: v.tensor_mul(
                bufD[:, _e * 512:(_e + 1) * 512], t1sc[:, _hf * 512:(_hf + 1) * 512], ps_a[:, :]),
                waits=[("t", br2)] + (P.bw("bufD") if e == 0 else []), inc="v")
            P.bset(f"t1sc{hf}", ("v", ym))
            ymark[e] = ym
        y2_done = ym
        P.bset("bank_a", ("v", y2_done))
        P.bset("bank_b", ("v", y2_done))
        for oc in range(8):
            bank = (ps_c, ps_d, ps_e, ps_f)[oc % 4]
            bkey = "bank_" + BK[id(bank)]

            tq = None
            for kc in range(8):
                tq = P.op("tensor", lambda t, _oc=oc, _kc=kc, _bank=bank: t.matmul(
                    _bank[:, :], wchunk(wq, _oc, _kc), bufD[:, _kc * 512:(_kc + 1) * 512],
                    start=(_kc == 0), stop=(_kc == 7)),
                    waits=([("v", ymark[kc]), ("lg", wqm)] if oc == 0 else []) +
                          (P.bw(bkey) if kc == 0 else []),
                    inc="t" if kc == 7 else None)
            sq_ = P.op("scalar", lambda h, _oc=oc, _bank=bank: h.activation(
                out=eqt[:, (_oc % 2) * 512:(_oc % 2 + 1) * 512], in_=_bank[:, :], func=AF.Identity,
                bias=bcol[:, _oc:_oc + 1], scale=1.0),
                waits=[("t", tq)] + P.bw(f"qch{oc % 2}") + (P.bw("eqt") if oc < 2 else []), inc="s")
            stq = P.op("sync", lambda s, _it=it, _oc=oc: s.dma_start(
                out=q_hbm.rearrange("(o p) t -> p o t", p=128)[:, _oc, _it * TS:(_it + 1) * TS],
                in_=eqt[:, (_oc % 2) * 512:(_oc % 2 + 1) * 512]),
                waits=[("s", sq_)], inc="st")
            st_q[(it, oc)] = stq
            P.bset(f"qch{oc % 2}", ("st", stq))

            def k_mm(t, _oc=oc, _bank=bank):
                last = None
                for kc in range(8):
                    last = t.matmul(_bank[:, :], wchunk(wk, _oc, kc), bufD[:, kc * 512:(kc + 1) * 512],
                                    start=(kc == 0), stop=(kc == 7))
                return last
            tk = P.op("tensor", k_mm,
                      waits=[("s", sq_)] + ([("lg", wkm)] if oc == 0 else []) +
                            (P.bw("kt") if oc == 0 else []), inc="t")
            ssq = P.op("scalar", lambda h, _oc=oc, _bank=bank: h.activation(
                out=sqch[:, (_oc % 2) * 512:(_oc % 2 + 1) * 512], in_=_bank[:, :],
                func=AF.Square, bias=bcol[:, 8 + _oc:9 + _oc], scale=1.0),
                waits=[("t", tk)] + (P.bw("sqch") if oc < 2 else [("t", P.cnt["t"])]), inc="s")
            sk_ = P.op("scalar", lambda h, _oc=oc, _bank=bank: h.activation(
                out=kt[:, _oc * 512:(_oc + 1) * 512], in_=_bank[:, :], func=AF.Identity,
                bias=bcol[:, 8 + _oc:9 + _oc], scale=1.0), inc="s")
            P.bset(bkey, ("s", sk_))
            P.op("tensor", lambda t, _oc=oc: t.matmul(
                ps_h[0:16, :], sel16[:, _oc * 16:(_oc + 1) * 16],
                sqch[:, (_oc % 2) * 512:(_oc % 2 + 1) * 512],
                start=(_oc == 0), stop=(_oc == 7)),
                waits=[("s", ssq)] + (P.bw("bank_h") if oc == 0 else []), inc="t")
        P.bset("sqch", ("t", P.cnt["t"]))
        dg_v = P.op("vector", lambda v: v.tensor_copy(dgrows[:, :], ps_h[0:16, :]),
                    waits=[("t", P.cnt["t"])], inc="v")
        P.bset("bank_h", ("v", dg_v))
        vdone = {}
        for sub in range(NSUB):
            for half in range(2):
                bank = (ps_c, ps_d, ps_e, ps_f)[(sub * 2 + half) % 4]
                bkey = "bank_" + BK[id(bank)]

                def v_mm(t, _sub=sub, _hf=half, _bank=bank):
                    last = None
                    for kc in range(8):
                        last = t.matmul(_bank[:, :],
                                        bufD[:, kc * 512 + _sub * 128: kc * 512 + (_sub + 1) * 128],
                                        wv_moving(kc, _hf), start=(kc == 0), stop=(kc == 7))
                    return last
                tv = P.op("tensor", v_mm,
                          waits=([("lg", wvm)] if (sub == 0 and half == 0) else []) + P.bw(bkey) +
                                (P.bw("v1t") if (sub == 0 and half == 0) else []), inc="t")
                ve = P.op("vector", lambda v, _sub=sub, _hf=half, _bank=bank: v.scalar_tensor_tensor(
                    out=v1t[:].rearrange("p (s h c) -> p s h c", s=4, h=16)[
                        :, _sub, _hf * 8:(_hf + 1) * 8, 0:64],
                    in0=_bank[:].rearrange("p (h c) -> p h c", h=8),
                    scalar=1.0, in1=bvb[:, _hf * 512:(_hf + 1) * 512].rearrange(
                        "p (h c) -> p h c", h=8),
                    op0=ALU.mult, op1=ALU.add),
                    waits=[("t", tv)] + ([("lg", bvbm)] if (sub == 0 and half == 0) else []), inc="v")
                P.bset(bkey, ("v", ve))
                vdone[(sub, half)] = ve
        ddbanks = [(ps_e, ps_f), (ps_c, ps_d)]

        def emit_dctr(_sub):
            _par = _sub % 2
            _dc = _par * 16
            tt = P.op("tensor", lambda t: t.transpose(
                ps_b[:, 0:16], dgrows[:, _sub * 128:(_sub + 1) * 128], idn_f[0:16, 0:16]),
                waits=[("v", dg_v)] + P.bw("bank_b"), inc="t")
            dcv = P.op("vector", lambda v: v.tensor_copy(
                dcols[:, _dc:_dc + 16], ps_b[:, 0:16]),
                waits=[("t", tt)] + P.bw(f"dcols{_par}"), inc="v")
            P.bset("bank_b", ("v", dcv))
            dctr[_sub] = dcv
        dctr = {}
        emit_dctr(0)
        for sub in range(NSUB):
            par = sub % 2
            eko = 0
            dco = par * 16
            dc_v = dctr[sub]
            expm = {}
            last_c = None

            def emit_c(h_, _sub=sub, _eko=eko):
                nonlocal last_c
                bank = (ps_g, ps_h)[h_ % 2]
                bkey = "bank_" + BK[id(bank)]
                tc_ = P.op("tensor", lambda t, _h=h_, _s=_sub, _e=_eko, _bank=bank: t.matmul(
                    _bank[0:65, 0:257], v1t[:, _s * 1040 + _h * 65:_s * 1040 + (_h + 1) * 65],
                    ekt[:, _e + _h * 257:_e + (_h + 1) * 257], start=True, stop=True),
                    waits=[("s", expm[h_]), ("v", vdone[(_sub, 1)])] + P.bw(bkey), inc="t")
                fl = P.op("vector", lambda v, _h=h_, _bank=bank: v.tensor_add(
                    cacc[0:65, _h * 257:(_h + 1) * 257], cacc[0:65, _h * 257:(_h + 1) * 257],
                    _bank[0:65, 0:257]), waits=[("t", tc_)], inc="v")
                P.bset(bkey, ("v", fl))
                last_c = tc_

            for hp in range(8):
                bA, bB = ddbanks[hp % 2]
                kA, kB = "bank_" + BK[id(bA)], "bank_" + BK[id(bB)]
                hA, hB = 2 * hp, 2 * hp + 1
                base = hp * 512 + sub * 128
                tA = P.op("tensor", lambda t, _b=base, _bA=bA: t.matmul(
                    _bA[:, 0:256], kt[0:64, _b:_b + 128], proj2b[0:64, :],
                    start=True, stop=True),
                    waits=P.bw(kA) + ([("v", dc_v)] if hp == 0 else []), inc="t")
                tB = P.op("tensor", lambda t, _b=base, _bB=bB: t.matmul(
                    _bB[:, 0:256], kt[64:128, _b:_b + 128], proj2b[64:128, :],
                    start=True, stop=True, tile_position=(64, 0)),
                    waits=P.bw(kB), inc="t")
                ekw = P.bw("ekt") if hp == 0 else []
                eA = P.op("scalar", lambda h, _hA=hA, _bA=bA, _e=eko, _dc=dco: h.activation(
                    out=ekt[:, _e + _hA * 257:_e + _hA * 257 + 256], in_=_bA[:, 0:256], func=AF.Exp,
                    bias=dcols[:, _dc + _hA:_dc + _hA + 1], scale=1.0),
                    waits=[("t", tA)] + ekw, inc="s")
                P.bset(kA, ("s", eA))
                eB = P.op("scalar", lambda h, _hB=hB, _bB=bB, _e=eko, _dc=dco: h.activation(
                    out=ekt[:, _e + _hB * 257:_e + _hB * 257 + 256], in_=_bB[:, 0:256], func=AF.Exp,
                    bias=dcols[:, _dc + _hB:_dc + _hB + 1], scale=1.0),
                    waits=[("t", tB)], inc="s")
                P.bset(kB, ("s", eB))
                expm[hA] = eA
                expm[hB] = eB
                if hp == 4:
                    vr1 = P.op("vector", lambda v, _dc=dco: v.reduce_max(
                        lnb[:, _dc:_dc + 8].rearrange("p (h o) -> p h o", h=8),
                        ekt[:, 0:2056].rearrange("p (h c) -> p h c", h=8)[:, :, 0:256],
                        axis=mybir.AxisListType.X), waits=[("s", expm[7])], inc="v")
                if hp >= 1:
                    emit_c(2 * (hp - 1))
                    emit_c(2 * (hp - 1) + 1)
            ek_done = P.cnt["s"]
            if sub + 1 < NSUB:
                emit_dctr(sub + 1)
            vr2_ = P.op("vector", lambda v, _dc=dco: v.reduce_max(
                lnb[:, _dc + 8:_dc + 16].rearrange("p (h o) -> p h o", h=8),
                ekt[:, 2056:4112].rearrange("p (h c) -> p h c", h=8)[:, :, 0:256],
                axis=mybir.AxisListType.X), waits=[("s", ek_done)], inc="v")
            v_rm = vr2_
            s_ln = P.op("scalar", lambda h, _dc=dco: h.activation(
                out=lnb[:, _dc:_dc + 16], in_=lnb[:, _dc:_dc + 16],
                func=AF.Ln, bias=0.0, scale=1.0),
                waits=[("v", v_rm)], inc="s")
            v_sb = P.op("vector", lambda v, _dc=dco: v.tensor_sub(
                lnb[:, _dc:_dc + 16], lnb[:, _dc:_dc + 16], dcols[:, _dc:_dc + 16]),
                waits=[("s", s_ln)], inc="v")
            vsm = P.op("vector", lambda v, _dc=dco: v.tensor_max(
                smax16[:, :], smax16[:, :], lnb[:, _dc:_dc + 16]), inc="v")
            P.bset(f"dcols{par}", ("v", vsm))
            emit_c(14)
            emit_c(15)
            P.bset("ekt", ("t", last_c), ("v", v_rm))
        P.bset("v1t", ("t", P.cnt["t"]))
        P.bset("kt", ("t", P.cnt["t"]))
        P.bset("bufD", ("t", P.cnt["t"]))
    v_f1 = P.op("vector", lambda v: v.reduce_max(vcol[:, 0:1], smax16[:, :],
                                                 axis=mybir.AxisListType.X), inc="v")
    t_f1 = P.op("tensor", lambda t: t.transpose(ps_g[0:1, 0:128], vcol[:, 0:1], idn_f[:, :]),
                waits=[("v", v_f1)] + P.bw("bank_g"), inc="t")
    v_f2 = P.op("vector", lambda v: v.memset(stg[:, :], 0.0), inc="v")
    v_f3 = P.op("vector", lambda v: v.reduce_max(stg[0:1, 0:1], ps_g[0:1, 0:128],
                                                 axis=mybir.AxisListType.X),
                waits=[("t", t_f1), ("v", v_f2)], inc="v")
    P.bset("bank_g", ("v", v_f3))

    prev_out_store = None
    cload = {}

    def emit_lq(_it):
        lq = P.op("gpsimd", lambda g: g.dma_start(
            out=kt[:].rearrange("p (o t) -> p o t", o=8),
            in_=q_hbm.rearrange("(o p) t -> p o t", p=128)[:, :, _it * TS:(_it + 1) * TS]),
            waits=([("st", st_q[(NT - 1, 7)])] if _it == 0 else []) + P.bw("kt"), inc="lg")
        cload[_it] = lq

    def emit_lx1(_it, extra):
        lx1 = P.op("gpsimd", lambda g: g.dma_start(
            out=bufB[:].rearrange("p (e t) -> p e t", e=8),
            in_=x1_hbm.rearrange("(e p) t -> p e t", p=128)[:, :, _it * TS:(_it + 1) * TS]),
            waits=list(extra) + [("st", st_x1[_it])], inc="lg")
        cload[(_it, "x1")] = lx1

    d2 = P.op("gpsimd", lambda g: g.dma_start(out=stab_in[0:1, 0:64], in_=stg[:, :]),
              waits=[("v", v_f3)], inc="lg")
    d1a = P.op("gpsimd", lambda g: g.dma_start(out=p_in_a[:, :], in_=cacc[0:65, 0:2056]),
               waits=[("v", P.cnt["v"])], inc="lg")
    d1b = P.op("gpsimd", lambda g: g.dma_start(out=p_in_b[:, :], in_=cacc[0:65, 2056:4112]),
               inc="lg")
    emit_lq(0)
    emit_lx1(0, [])
    cc2 = P.op("gpsimd", lambda g: g.collective_compute(
        "AllReduce", ALU.max, replica_groups=[[0, 1, 2, 3, 4, 5, 6, 7]],
        ins=[stab_in.ap().opt()], outs=[stab_out.ap().opt()]), waits=[("lg", d2)], inc="cc")
    cc1a = P.op("gpsimd", lambda g: g.collective_compute(
        "AllReduce", ALU.add, replica_groups=[[0, 1, 2, 3], [4, 5, 6, 7]],
        ins=[p_in_a.ap().opt()], outs=[p_out_a.ap().opt()]), waits=[("lg", d1a)], inc="cc")
    cc1b = P.op("gpsimd", lambda g: g.collective_compute(
        "AllReduce", ALU.add, replica_groups=[[0, 1, 2, 3], [4, 5, 6, 7]],
        ins=[p_in_b.ap().opt()], outs=[p_out_b.ap().opt()]), waits=[("lg", d1b)], inc="cc")
    d4 = P.op("gpsimd", lambda g: g.dma_start(out=stg[:, :], in_=stab_out[0:1, 0:64]),
              waits=[("cc", cc2)], inc="lg")
    d3a = P.op("gpsimd", lambda g: g.dma_start(out=cacc[0:65, 0:2056], in_=p_out_a[:, :]),
               waits=[("cc", cc1a)], inc="lg")
    d3b = P.op("gpsimd", lambda g: g.dma_start(out=cacc[0:65, 2056:4112], in_=p_out_b[:, :]),
               waits=[("cc", cc1b)], inc="lg")
    _asm = {}

    def emit_assembly():
        s_s1 = P.op("scalar", lambda h: h.activation(out=scrow[:, 24:32], in_=stg[0:1, 0:8],
                                                     func=AF.Exp, bias=0.0, scale=-1.0),
                    waits=[("lg", d4)], inc="s")
        t_s1 = P.op("tensor", lambda t: t.matmul(ps_a[:, 0:4], ones_f[0:1, 0:128], scrow[0:1, 24:28],
                                                 start=True, stop=True),
                    waits=[("s", s_s1)] + P.bw("bank_a"), inc="t")
        v_s1 = P.op("vector", lambda v: v.tensor_copy(smallcol[:, 12:16], ps_a[:, 0:4]),
                    waits=[("t", t_s1)], inc="v")
        P.bset("bank_a", ("v", v_s1))
        P.op("sync", lambda s: s.dma_start(out=dbg[0:65, :], in_=cacc[0:65, :]),
             waits=[("lg", d3b)], inc="st")
        csx_t = None
        for h_ in range(16):
            dwait = d3a if h_ < 8 else d3b
            tv1 = P.op("tensor", lambda t, _h=h_: t.transpose(
                ps_b[0:1, 0:65], cacc[0:65, _h * 257 + 256:_h * 257 + 257], idn_f[0:65, 0:65]),
                waits=[("lg", dwait)] + P.bw("bank_b"), inc="t")
            vv1 = P.op("vector", lambda v: v.tensor_scalar_mul(rows[:, 0:65], ps_b[0:1, 0:65], EPS),
                       waits=[("t", tv1)], inc="v")
            vv1b = P.op("vector", lambda v: v.memset(rows[:, 64:65], EPS * NTOK), inc="v")
            P.bset("bank_b", ("v", vv1b))
            tb = P.op("tensor", lambda t: t.matmul(ps_b[:, 0:65], ones_f[0:1, 0:128], rows[0:1, 0:65],
                                                   start=True, stop=True),
                      waits=[("v", vv1b)] + P.bw("bank_b"), inc="t")
            vb = P.op("vector", lambda v: v.tensor_copy(rcb[:, 0:65], ps_b[:, 0:65]),
                      waits=[("t", tb)], inc="v")
            P.bset("bank_b", ("v", vb))
            cm = []
            for c in range(2):
                ta = P.op("tensor", lambda t, _h=h_, _c=c: t.transpose(
                    ps_a[:, 0:65], cacc[0:65, _h * 257 + _c * 128:_h * 257 + (_c + 1) * 128],
                    idn_f[0:65, 0:65]), waits=P.bw("bank_a"), inc="t")
                vc_ = P.op("vector", lambda v, _h=h_, _c=c: v.scalar_tensor_tensor(
                    out=ctxden[:, (_h * 2 + _c) * 65:(_h * 2 + _c + 1) * 65], in0=ps_a[:, 0:65],
                    scalar=smallcol[:, 12:13], in1=rcb[:, 0:65], op0=ALU.mult, op1=ALU.add),
                    waits=[("t", ta), ("v", vb), ("v", v_s1)], inc="v")
                P.bset("bank_a", ("v", vc_))
                cm.append(vc_)
            for c in range(2):
                csx_t = P.op("tensor", lambda t, _h=h_, _c=c: t.matmul(
                    ps_h[0:16, 0:65], selh[:, _h * 16:(_h + 1) * 16],
                    ctxden[:, (_h * 2 + _c) * 65:(_h * 2 + _c + 1) * 65],
                    start=(_h == 0 and _c == 0), stop=(_h == 15 and _c == 1)),
                    waits=[("v", cm[c])] + (P.bw("bank_h") if (h_ == 0 and c == 0) else []),
                    inc="t")
        vcs = P.op("vector", lambda v: v.tensor_mul(
            csxt[:].rearrange("p (h c) -> p h c", h=16), maskt[:].rearrange("p (h c) -> p h c", h=16),
            ps_h[0:16, 0:65].unsqueeze(1).broadcast_to([16, 16, 65])),
            waits=[("t", csx_t)], inc="v")
        P.bset("bank_h", ("v", vcs))
        _asm["vcs"] = vcs


    for it in range(NT):
        lq = cload[it]
        lx1 = cload[(it, "x1")]
        for oc in range(8):
            hf = oc % 2
            if oc % 2 == 0:
                ssq = P.op("vector", lambda v, _oc=oc, _hf=hf: v.tensor_mul(
                    sqch[:, _hf * 512:(_hf + 1) * 512], kt[:, _oc * 512:(_oc + 1) * 512],
                    kt[:, _oc * 512:(_oc + 1) * 512]),
                    waits=[("lg", lq)] + (P.bw("sqch") if oc < 2 else [("t", P.cnt["t"])]), inc="v")
                sqw = ("v", ssq)
            else:
                ssq = P.op("scalar", lambda h, _oc=oc, _hf=hf: h.activation(
                    out=sqch[:, _hf * 512:(_hf + 1) * 512], in_=kt[:, _oc * 512:(_oc + 1) * 512],
                    func=AF.Square, bias=0.0, scale=1.0),
                    waits=[("lg", lq)] + (P.bw("sqch") if oc < 2 else [("t", P.cnt["t"])]), inc="s")
                sqw = ("s", ssq)
            P.op("tensor", lambda t, _oc=oc, _hf=hf: t.matmul(
                ps_h[0:16, :], sel16[:, _oc * 16:(_oc + 1) * 16],
                sqch[:, _hf * 512:(_hf + 1) * 512],
                start=(_oc == 0), stop=(_oc == 7)),
                waits=[sqw] + (P.bw("bank_h") if oc == 0 else []), inc="t")
        P.bset("sqch", ("t", P.cnt["t"]))
        dg_v = P.op("vector", lambda v: v.tensor_copy(dgrows[:, :], ps_h[0:16, :]),
                    waits=[("t", P.cnt["t"])] + P.bw("dgrows"), inc="v")
        P.bset("bank_h", ("v", dg_v))
        rmbanks = [(ps_e, ps_f), (ps_c, ps_d)]
        for sub in range(NSUB):
            for hp in range(8):
                bA, bB = rmbanks[hp % 2]
                kA, kB = "bank_" + BK[id(bA)], "bank_" + BK[id(bB)]
                hA, hB = 2 * hp, 2 * hp + 1
                base = hp * 512 + sub * 128
                tA = P.op("tensor", lambda t, _b=base, _bA=bA: t.matmul(
                    _bA[:, 0:256], kt[0:64, _b:_b + 128], proj2b[0:64, :],
                    start=True, stop=True), waits=P.bw(kA), inc="t")
                tB = P.op("tensor", lambda t, _b=base, _bB=bB: t.matmul(
                    _bB[:, 0:256], kt[64:128, _b:_b + 128], proj2b[64:128, :],
                    start=True, stop=True, tile_position=(64, 0)), waits=P.bw(kB), inc="t")
                rA = P.op("vector", lambda v, _hA=hA, _bA=bA: v.reduce_max(
                    rmcols[:, _hA:_hA + 1], _bA[:, 0:256], axis=mybir.AxisListType.X),
                    waits=[("t", tA)] + (P.bw("rmcols") if (sub == 0 and hp == 0) else []), inc="v")
                P.bset(kA, ("v", rA))
                rB = P.op("vector", lambda v, _hB=hB, _bB=bB: v.reduce_max(
                    rmcols[:, _hB:_hB + 1], _bB[:, 0:256], axis=mybir.AxisListType.X),
                    waits=[("t", tB)], inc="v")
                P.bset(kB, ("v", rB))
            rm_v = P.cnt["v"]
            trm = P.op("tensor", lambda t: t.transpose(ps_g[0:16, 0:128], rmcols[:, :], idn_f[:, :]),
                       waits=[("v", rm_v)] + P.bw("bank_g"), inc="t")
            vtm = P.op("vector", lambda v, _sub=sub: v.tensor_copy(
                rmT[:, _sub * 128:(_sub + 1) * 128], ps_g[0:16, 0:128]),
                waits=[("t", trm)], inc="v")
            P.bset("bank_g", ("v", vtm))
        ve1 = P.op("vector", lambda v: v.scalar_tensor_tensor(
            out=t16[:, :], in0=rmT[:, :], scalar=-K0, in1=dgrows[:, :],
            op0=ALU.add, op1=ALU.subtract),
            waits=[("v", P.cnt["v"])], inc="v")
        P.bset("dgrows", ("v", ve1))
        P.bset("rmcols", ("v", ve1))
        se1 = P.op("scalar", lambda h: h.activation(out=eps16[:, :], in_=t16[:, :],
                                                    func=AF.Exp, bias=0.0, scale=1.0),
                   waits=[("v", ve1)] + P.bw("eps16"), inc="s")
        if it == 0:
            emit_assembly()
        vcs = _asm["vcs"]
        pending_ln = []
        pending_div = []
        for hp in range(8):
            while pending_div:
                pending_div.pop(0)()
            hA, hB = 2 * hp, 2 * hp + 1
            qbase = hp * 512
            banksA = (ps_c, ps_d)
            banksB = (ps_e, ps_f)
            em = []
            for c in range(2):
                kA = "bank_" + BK[id(banksA[c])]
                kB = "bank_" + BK[id(banksB[c])]
                tfa = P.op("tensor", lambda t, _c=c, _qb=qbase, _bk=banksA: t.matmul(
                    _bk[_c][:, :], proj2b[0:64, _c * 128:(_c + 1) * 128], kt[0:64, _qb:_qb + 512],
                    start=True, stop=True), waits=P.bw(kA), inc="t")
                ea = P.op("scalar", lambda h, _c=c, _bk=banksA: h.activation(
                    out=eqt[:, _c * 512:(_c + 1) * 512], in_=_bk[_c][:, :], func=AF.Exp,
                    bias=ck[:, 0:1], scale=1.0),
                    waits=[("t", tfa)] + (P.bw("eqt") if (hp == 0 and c == 0) else []), inc="s")
                P.bset(kA, ("s", ea))
                tfb = P.op("tensor", lambda t, _c=c, _qb=qbase, _bk=banksB: t.matmul(
                    _bk[_c][:, :], proj2b[64:128, _c * 128:(_c + 1) * 128], kt[64:128, _qb:_qb + 512],
                    start=True, stop=True, tile_position=(64, 0)), waits=P.bw(kB), inc="t")
                eb = P.op("scalar", lambda h, _c=c, _bk=banksB: h.activation(
                    out=eqt[:, 1024 + _c * 512:1024 + (_c + 1) * 512], in_=_bk[_c][:, :], func=AF.Exp,
                    bias=ck[:, 0:1], scale=1.0), waits=[("t", tfb)], inc="s")
                P.bset(kB, ("s", eb))
                em.append((ea, eb))
            if pending_ln:
                pending_div.append(pending_ln.pop(0)())

            def numden(_h, _eqoff, _bank, _bkey, _em, extra):
                P.op("tensor", lambda t: t.matmul(
                    _bank[0:65, :], ctxden[:, (_h * 2) * 65:(_h * 2 + 1) * 65],
                    eqt[:, _eqoff:_eqoff + 512], start=True, stop=False),
                    waits=[("s", _em[0]), ("s", se1)] + P.bw(_bkey) + extra)
                P.op("tensor", lambda t: t.matmul(
                    _bank[0:65, :], ctxden[:, (_h * 2 + 1) * 65:(_h * 2 + 2) * 65],
                    eqt[:, _eqoff + 512:_eqoff + 1024], start=False, stop=False),
                    waits=[("s", _em[1])])
                return P.op("tensor", lambda t: t.matmul(
                    _bank[0:65, :], csxt[:, _h * 65:(_h + 1) * 65],
                    eps16[:, :], start=False, stop=True), inc="t")
            tnA = numden(hA, 0, ps_g, "bank_g", [em[0][0], em[1][0]],
                         [("v", vcs)] if (it == 0 and hp == 0) else [])
            tnB = numden(hB, 1024, ps_h, "bank_h", [em[0][1], em[1][1]], [])
            slA = (hp % 2) * 1024
            slB = (hp % 2) * 1024 + 512
            cpA = P.op("vector", lambda v, _sl=slA: v.tensor_copy(
                bufA[0:65, _sl:_sl + 512], ps_g[0:65, :]),
                waits=[("t", tnA)] + P.bw(f"nsl{hp % 2}A"), inc="v")
            P.bset("bank_g", ("v", cpA))
            cpB = P.op("vector", lambda v, _sl=slB: v.tensor_copy(
                bufA[0:65, _sl:_sl + 512], ps_h[0:65, :]),
                waits=[("t", tnB)] + P.bw(f"nsl{hp % 2}B"), inc="v")
            P.bset("bank_h", ("v", cpB))
            def _emit_ln(_hp=hp, _cpB=cpB, _slA=slA, _slB=slB, _it=it):
                sl0 = min(_slA, _slB)
                par = _hp % 2
                rowbuf = sqch if par == 0 else rcb
                P.op("scalar", lambda h, _s0=sl0: h.activation(
                    out=t1sc[64:65, 0:1024], in_=bufA[64:65, _s0:_s0 + 1024], func=AF.Ln,
                    bias=0.0, scale=1.0),
                    waits=[("v", _cpB)] +
                          ((P.bw("t1sc0") + P.bw("t1sc1")) if _hp == 0 else []), inc="s")
                scAB = P.op("scalar", lambda h, _rb=rowbuf: h.activation(
                    out=_rb[64:65, 0:1024], in_=t1sc[64:65, 0:1024], func=AF.Exp, bias=0.0,
                    scale=-1.0),
                    waits=P.bw(f"drow{par}") +
                          (P.bw("sqch") if (par == 0 and _hp < 2) else []) +
                          ([("v", _asm["vcs"])] if (_it == 0 and _hp == 1) else []),
                    inc="s")

                def _emit_div(_rb=rowbuf, _par=par):
                    bA_ = P.op("tensor", lambda t: t.matmul(ps_a[0:64, :], ones_b[64:65, 0:64],
                                                            _rb[64:65, 0:512], start=True, stop=True,
                                                            tile_position=(64, 0)),
                               waits=[("s", scAB)] + P.bw("bank_a"), inc="t")
                    vrA = P.op("vector", lambda v: v.tensor_copy(rcb[0:64, 0:512], ps_a[0:64, :]),
                               waits=[("t", bA_)] + P.bw("rcbA"), inc="v")
                    P.bset("bank_a", ("v", vrA))
                    evA = P.op("vector", lambda v: v.tensor_mul(
                        bufD[0:64, _hp * 512:(_hp + 1) * 512], bufA[0:64, _slA:_slA + 512],
                        rcb[0:64, 0:512]),
                        waits=[("v", vrA)] + (P.bw("bufDc") if _hp == 0 else []), inc="v")
                    P.bset(f"nsl{_hp % 2}A", ("v", evA))
                    P.bset("rcbA", ("v", evA))
                    bB_ = P.op("tensor", lambda t: t.matmul(ps_b[0:64, :], ones_b[64:65, 0:64],
                                                            _rb[64:65, 512:1024], start=True,
                                                            stop=True, tile_position=(64, 0)),
                               waits=[("s", scAB)] + P.bw("bank_b"), inc="t")
                    P.bset(f"drow{_par}", ("t", bB_))
                    vrB = P.op("vector", lambda v: v.tensor_copy(rcb[0:64, 512:1024], ps_b[0:64, :]),
                               waits=[("t", bB_)] + P.bw("rcbB"), inc="v")
                    P.bset("bank_b", ("v", vrB))
                    evB = P.op("vector", lambda v: v.tensor_mul(
                        attnst[0:64, _hp * 512:(_hp + 1) * 512], bufA[0:64, _slB:_slB + 512],
                        rcb[0:64, 512:1024]),
                        waits=[("v", vrB)], inc="v")
                    P.bset(f"nsl{_hp % 2}B", ("v", evB))
                    P.bset("rcbB", ("v", evB))
                    mvp = P.op("sync", lambda sy: sy.dma_start(
                        out=bufD[64:128, _hp * 512:(_hp + 1) * 512],
                        in_=attnst[0:64, _hp * 512:(_hp + 1) * 512]),
                        waits=[("v", evB)] + (P.bw("bufDmv") if _hp == 0 else []), inc="st")
                    P.bset("mvlast", ("st", mvp))
                return _emit_div
            pending_ln.append(_emit_ln)
        while pending_ln:
            pending_div.append(pending_ln.pop(0)())
        while pending_div:
            pending_div.pop(0)()
        kt_free_t = P.cnt["t"]
        kt_free_s = P.cnt["s"]
        P.bset("kt", ("t", kt_free_t), ("s", kt_free_s), ("v", P.cnt["v"]))
        if it + 1 < NT:
            emit_lq(it + 1)
        P.bset("eqt", ("t", P.cnt["t"]))
        P.bset("sqch", ("t", P.cnt["t"]))
        attn_v = P.cnt["v"]
        mv = P.bw("mvlast")[0][1] if P.bw("mvlast") else None
        for oc in range(8):
            bank = (ps_c, ps_d, ps_e, ps_f)[oc % 4]
            bkey = "bank_" + BK[id(bank)]

            def wo_mm(t, _oc=oc, _bank=bank):
                last = None
                for kc in range(8):
                    last = t.matmul(_bank[:, :], wchunk(wo, _oc, kc), bufD[:, kc * 512:(kc + 1) * 512],
                                    start=(kc == 0), stop=(kc == 7))
                return last
            tw = P.op("tensor", wo_mm,
                      waits=(([("st", mv)] if mv else []) + [("lg", wom)] if oc == 0 else []) + P.bw(bkey), inc="t")
            sw_ = P.op("scalar", lambda h, _oc=oc, _bank=bank: h.activation(
                out=t1sc[:, (_oc % 2) * 512:(_oc % 2 + 1) * 512], in_=_bank[:, :], func=AF.Identity,
                bias=bcol[:, 24 + _oc:25 + _oc], scale=1.0),
                waits=[("t", tw)] + P.bw(f"t1sc{oc % 2}"), inc="s")
            P.bset(bkey, ("s", sw_))
            va = P.op("vector", lambda v, _oc=oc: v.tensor_add(
                bufB[:, _oc * 512:(_oc + 1) * 512], bufB[:, _oc * 512:(_oc + 1) * 512],
                t1sc[:, (_oc % 2) * 512:(_oc % 2 + 1) * 512]),
                waits=[("s", sw_), ("lg", lx1)] +
                      ([("st", prev_out_store)] if (oc == 0 and prev_out_store) else []), inc="v")
            P.bset(f"t1sc{oc % 2}", ("v", va))
        x2_done = P.cnt["v"]
        vr3, _ = ln_stats(bufB, want_rinvs=True)
        bm3 = bcast_b(rows_b[0:1, 0:512], ps_b, "bank_b", wait_marks=[("v", P.cnt["v"])])
        br3 = bcast_f(rows[0:1, 1024:1536], ps_a, "bank_a", wait_marks=[("v", vr3)])
        qm3 = None
        qmark3 = {}
        for e in range(8):
            hf = e % 2
            P.op("vector", lambda v, _e=e, _hf=hf: v.scalar_tensor_tensor(
                out=t1sc[:, _hf * 512:(_hf + 1) * 512], in0=bufB[:, _e * 512:(_e + 1) * 512],
                scalar=1.0, in1=ps_b[:, :], op0=ALU.mult, op1=ALU.subtract),
                waits=[("t", bm3)] + P.bw(f"t1sc{hf}"), inc="v")
            qm3 = P.op("vector", lambda v, _e=e, _hf=hf: v.tensor_mul(
                q8t[:, _e * 512:(_e + 1) * 512], t1sc[:, _hf * 512:(_hf + 1) * 512], ps_a[:, :]),
                waits=[("t", br3)], inc="v")
            P.bset(f"t1sc{hf}", ("v", qm3))
            qmark3[e] = qm3
        P.bset("bank_a", ("v", qm3))
        P.bset("bank_b", ("v", qm3))
        for oc in range(8):
            bank = (ps_c, ps_d, ps_e, ps_f)[oc % 4]
            bkey = "bank_" + BK[id(bank)]
            tm = None
            for kp in range(4):
                tm = P.op("tensor", lambda t, _oc=oc, _kp=kp, _bank=bank: t.matmul(
                    _bank[:, :],
                    wchunk(w8, _oc, 2 * _kp, width=256).rearrange("p (k c) -> p k c", k=2),
                    q8t[:, _kp * 1024:(_kp + 1) * 1024].rearrange("p (k n) -> p k n", k=2),
                    start=(_kp == 0), stop=(_kp == 3),
                    perf_mode=mybir.MatmulPerfMode.DoubleRow),
                    waits=([("v", qmark3[2 * kp + 1])] if oc == 0 else []) +
                          (P.bw(bkey) if kp == 0 else []),
                    inc="t" if kp == 3 else None)
            sm = P.op("scalar", lambda h, _oc=oc, _bank=bank: h.activation(
                out=t1sc[:, (_oc % 2) * 512:(_oc % 2 + 1) * 512], in_=_bank[:, :], func=AF.Identity,
                bias=hbf[:, _oc:_oc + 1], scale=smallcol[:, 8:9]),
                waits=[("t", tm)] + P.bw(f"t1sc{oc % 2}"), inc="s")
            P.bset(bkey, ("s", sm))
            vo = P.op("vector", lambda v, _oc=oc: v.tensor_add(
                bufB[:, _oc * 512:(_oc + 1) * 512], bufB[:, _oc * 512:(_oc + 1) * 512],
                t1sc[:, (_oc % 2) * 512:(_oc % 2 + 1) * 512]),
                waits=[("s", sm)], inc="v")
            P.bset(f"t1sc{oc % 2}", ("v", vo))
        out_done = P.cnt["v"]
        P.bset("bufDc", ("v", out_done))
        so = P.op("sync", lambda s, _it=it: s.dma_start(
            out=xoT.rearrange("(e p) t -> p e t", p=128)[:, :, _it * TS:(_it + 1) * TS],
            in_=bufB[:].rearrange("p (e t) -> p e t", e=8)),
            waits=[("v", out_done)], inc="st")
        prev_out_store = so
        if it + 1 < NT:
            emit_lx1(it + 1, [("st", so), ("v", out_done)])
        P.bset("eps16", ("t", P.cnt["t"]))
        P.bset("bufD", ("t", P.cnt["t"]))
        P.bset("bufDmv", ("t", P.cnt["t"]))
    final_st = P.cnt["st"]

    with stack:
        with (
            nc.semaphore("ld") as s_ld,
            nc.semaphore("lg") as s_lg,
            nc.semaphore("st") as s_st,
            nc.semaphore("t") as s_t,
            nc.semaphore("v") as s_v,
            nc.semaphore("s") as s_s,
            nc.semaphore("g") as s_g,
            nc.semaphore("cc") as s_cc,
            nc.Block() as block,
        ):
            P.sem = {"ld": s_ld, "lg": s_lg, "st": s_st, "t": s_t, "v": s_v,
                     "s": s_s, "g": s_g, "cc": s_cc}

            @block.sync
            def _(h):
                for fn in P.ops["sync"]:
                    fn(h)
                h.wait_ge(s_st, final_st)

            @block.tensor
            def _(h):
                for fn in P.ops["tensor"]:
                    fn(h)

            @block.vector
            def _(h):
                for fn in P.ops["vector"]:
                    fn(h)

            @block.scalar
            def _(h):
                for fn in P.ops["scalar"]:
                    fn(h)

            @block.gpsimd
            def _(h):
                for fn in P.ops["gpsimd"]:
                    fn(h)
    return nc


def _prep_weight(w):
    return np.ascontiguousarray(
        w.reshape(8, 128, 8, 128).transpose(3, 0, 2, 1).reshape(128, 8192))


def _host_inputs(L0_attn_x, Wq, bq, Wk, bk, Wv, bv, Wo, bo, proj,
                 w8, bf_, iscale, wscale):
    base = {}
    base["wq_h"] = _prep_weight(np.asarray(Wq, np.float32))
    base["wk_h"] = _prep_weight(np.asarray(Wk, np.float32))
    base["wv_h"] = _prep_weight(np.asarray(Wv, np.float32))
    base["wo_h"] = _prep_weight(np.asarray(Wo, np.float32))
    base["w8_h"] = _prep_weight(np.asarray(w8, np.float32))
    base["proj_h"] = np.ascontiguousarray(
        np.concatenate([np.asarray(proj, np.float32).T] * 2, axis=0))
    bc = np.zeros((128, 40), np.float32)
    for i, b_ in enumerate([bq, bk, bv, bo, bf_]):
        bc[:, i * 8:(i + 1) * 8] = np.asarray(b_, np.float32).reshape(8, 128).T
    base["bcol_h"] = bc
    base["bvb_h"] = np.ascontiguousarray(
        np.broadcast_to(np.asarray(bv, np.float32)[None, :], (128, 1024)))
    sc = np.ones((1, 8), np.float32)
    sc[0, 0] = float(np.asarray(iscale).reshape(-1)[0])
    sc[0, 1] = float(np.asarray(wscale).reshape(-1)[0])
    base["sc_h"] = sc
    s16 = np.zeros((128, 128), np.float32)
    for c in range(8):
        s16[0:64, c * 16 + 2 * c] = NDIAG
        s16[64:128, c * 16 + 2 * c + 1] = NDIAG
    base["sel16_h"] = s16.astype(ml_dtypes.bfloat16)
    sh = np.zeros((128, 256), np.float32)
    for h in range(16):
        sh[:, h * 16 + h] = 1.0
    base["selh_h"] = sh.astype(ml_dtypes.bfloat16)
    base["ones_f_h"] = np.ones((128, 128), np.float32)
    base["ones_b_h"] = np.ones((128, 128), np.float32).astype(ml_dtypes.bfloat16)
    base["idn_f_h"] = np.eye(128, dtype=np.float32)
    ckv = np.zeros((128, 4), np.float32)
    ckv[:, 0] = -K0
    base["ck_h"] = ckv
    mk = np.zeros((16, 1040), np.float32)
    for h in range(16):
        mk[h, h * 65:(h + 1) * 65] = EPS
    base["maskt_h"] = mk.astype(ml_dtypes.bfloat16)
    return base


def kernel(L0_attn_x, L0_attn_Wq, L0_attn_bq, L0_attn_Wk, L0_attn_bk,
           L0_attn_Wv, L0_attn_bv, L0_attn_Wo, L0_attn_bo, L0_attn_proj,
           L0_ffn_weight_fp8, L0_ffn_bias, L0_ffn_input_scale, L0_ffn_weight_scale):
    x = np.asarray(L0_attn_x, np.float32)
    base = _host_inputs(x, L0_attn_Wq, L0_attn_bq, L0_attn_Wk, L0_attn_bk,
                        L0_attn_Wv, L0_attn_bv, L0_attn_Wo, L0_attn_bo,
                        L0_attn_proj, L0_ffn_weight_fp8, L0_ffn_bias,
                        L0_ffn_input_scale, L0_ffn_weight_scale)
    nc = build_program()
    in_maps = []
    for c in range(8):
        b_, tc = c // 4, c % 4
        xs = np.ascontiguousarray(x[b_, tc * T:(tc + 1) * T, :].T)
        m = dict(base)
        m["xT"] = xs
        in_maps.append(m)
    res = run_bass_kernel_spmd(nc, in_maps, core_ids=list(range(8)))
    out = np.empty((B, N, E), np.float32)
    for c in range(8):
        b_, tc = c // 4, c % 4
        out[b_, tc * T:(tc + 1) * T, :] = res.results[c]["xoT"].T
    return out


if __name__ == "__main__":
    z = np.load("/root/problem/exp/ref_data.npz")
    ins = {k: z[k] for k in z.files if k != "expected"}
    got = kernel(**ins)
    exp = z["expected"]
    d = got - exp
    rvr = float((d ** 2).mean() / (exp ** 2).mean())
    print(f"kernel vs reference: rvr={rvr:.3e} relL2={np.sqrt(rvr):.3e}")


# version 25 (final)
# speedup vs baseline: 1.0428x; 1.0428x over previous
import numpy as np
import ml_dtypes
import concourse.bass as bass
import concourse.mybir as mybir
from concourse.bass_utils import run_bass_kernel_spmd

F32 = mybir.dt.float32
BF16 = mybir.dt.bfloat16
F8 = mybir.dt.float8e4
AF = mybir.ActivationFunctionType
ALU = mybir.AluOpType

E, H, D, M = 1024, 16, 64, 256
B, N = 2, 16384
T = 4096
TS = 512
NT = T // TS
SUB = 128
NSUB = TS // SUB
EPS = 1e-4
K0 = 20.0
LNEPS = 1e-5
DSC = float(D ** -0.25)
NDIAG = -0.5 * float(D ** -0.5)
NTOK = float(N)

ENGS = ("sync", "tensor", "vector", "scalar", "gpsimd")
SEMS = ("ld", "lg", "st", "t", "v", "s", "g", "cc")


class Plan:
    def __init__(self):
        self.ops = {e: [] for e in ENGS}
        self.cnt = {s: 0 for s in SEMS}
        self.sem = {}
        self.bankw = {}
        self.hiwater = {}

    def op(self, eng, build, waits=(), inc=None, amt=None):
        after = None
        if inc is not None:
            if amt is None:
                amt = 16 if inc in ("ld", "lg", "st") else 1
            self.cnt[inc] += amt
            after = self.cnt[inc]
        hw = self.hiwater.setdefault(eng, {})
        w = []
        for (sname, v) in waits:
            if not v:
                continue
            if inc == sname:
                v2 = min(v, self.cnt[sname] - (amt or 1))
            else:
                v2 = v
            if v2 > hw.get(sname, -1):
                hw[sname] = v2
                w.append((sname, v2))
        w = tuple(w)

        def fn(h, _b=build, _w=w, _i=inc, _a=amt):
            for (sname, val) in _w:
                h.wait_ge(self.sem[sname], val)
            ins = _b(h)
            if _i is not None:
                assert ins is not None
                ins.then_inc(self.sem[_i], _a)
        self.ops[eng].append(fn)
        return after

    def bw(self, key):
        return list(self.bankw.get(key, []))

    def bset(self, key, *marks):
        self.bankw[key] = [m for m in marks if m is not None]


def build_program():
    nc = bass.Bass("TRN2", target_bir_lowering=False)
    P = Plan()

    xT = nc.dram_tensor("xT", [E, T], F32, kind="ExternalInput").ap()
    wq_h = nc.dram_tensor("wq_h", [128, 8192], F32, kind="ExternalInput").ap()
    wk_h = nc.dram_tensor("wk_h", [128, 8192], F32, kind="ExternalInput").ap()
    wv_h = nc.dram_tensor("wv_h", [128, 8192], F32, kind="ExternalInput").ap()
    wo_h = nc.dram_tensor("wo_h", [128, 8192], F32, kind="ExternalInput").ap()
    w8_h = nc.dram_tensor("w8_h", [128, 8192], F32, kind="ExternalInput").ap()
    proj_h = nc.dram_tensor("proj_h", [128, 256], F32, kind="ExternalInput").ap()
    bcol_h = nc.dram_tensor("bcol_h", [128, 40], F32, kind="ExternalInput").ap()
    bvb_h = nc.dram_tensor("bvb_h", [128, 1024], F32, kind="ExternalInput").ap()
    sc_h = nc.dram_tensor("sc_h", [1, 8], F32, kind="ExternalInput").ap()
    sel16_h = nc.dram_tensor("sel16_h", [128, 128], BF16, kind="ExternalInput").ap()
    selh_h = nc.dram_tensor("selh_h", [128, 256], BF16, kind="ExternalInput").ap()
    ones_f_h = nc.dram_tensor("ones_f_h", [128, 128], F32, kind="ExternalInput").ap()
    ones_b_h = nc.dram_tensor("ones_b_h", [128, 128], BF16, kind="ExternalInput").ap()
    idn_f_h = nc.dram_tensor("idn_f_h", [128, 128], F32, kind="ExternalInput").ap()
    ck_h = nc.dram_tensor("ck_h", [128, 4], F32, kind="ExternalInput").ap()
    maskt_h = nc.dram_tensor("maskt_h", [16, 1040], BF16, kind="ExternalInput").ap()

    xoT = nc.dram_tensor("xoT", [E, T], F32, kind="ExternalOutput").ap()
    x1_hbm = nc.dram_tensor("x1_hbm", [E, T], F32, kind="ExternalOutput").ap()
    q_hbm = nc.dram_tensor("q_hbm", [E, T], BF16, kind="ExternalOutput").ap()
    dbg = nc.dram_tensor("dbg", [128, 4112], F32, kind="ExternalOutput").ap()

    p_in_a = nc.dram_tensor("p_in_a", [65, 2056], F32)
    p_in_b = nc.dram_tensor("p_in_b", [65, 2056], F32)
    p_out_a = nc.dram_tensor("p_out_a", [65, 2056], F32)
    p_out_b = nc.dram_tensor("p_out_b", [65, 2056], F32)
    stab_in = nc.dram_tensor("stab_in", [1, 128], F32)
    stab_out = nc.dram_tensor("stab_out", [1, 128], F32)

    import contextlib
    stack = contextlib.ExitStack()
    sb = lambda name, shape, dt: stack.enter_context(nc.sbuf_tensor(name, shape, dt))
    ps = lambda name, shape: stack.enter_context(nc.psum_tensor(name, shape, F32))

    wq = sb("wq", [128, 8192], BF16)
    wk = sb("wk", [128, 8192], BF16)
    wv = sb("wv", [128, 8192], BF16)
    wo = sb("wo", [128, 8192], BF16)
    w8 = sb("w8", [128, 8192], F8)
    proj2b = sb("proj2b", [128, 256], BF16)
    sel16 = sb("sel16", [128, 128], BF16)
    selh = sb("selh", [128, 256], BF16)
    ones_f = sb("ones_f", [128, 128], F32)
    ones_b = sb("ones_b", [128, 128], BF16)
    idn_f = sb("idn_f", [128, 128], F32)
    ck = sb("ck", [128, 4], F32)
    bcol = sb("bcol", [128, 40], F32)
    hbf = sb("hbf", [128, 8], F32)
    bvb = sb("bvb", [128, 1024], BF16)
    scrow = sb("scrow", [1, 32], F32)
    smallcol = sb("smallcol", [128, 16], F32)
    rows = sb("rows", [1, 2048], F32)
    rows_b = sb("rows_b", [1, 512], BF16)
    bufA = sb("bufA", [128, 4096], F32)
    bufB = sb("bufB", [128, 4096], F32)
    bufD = sb("bufD", [128, 4096], BF16)
    kt = sb("kt", [128, 4096], BF16)
    q8t = sb("q8t", [128, 4096], F8)
    t1sc = sb("t1sc", [128, 1024], F32)
    sqch = sb("sqch", [128, 1024], BF16)
    v1t = sb("v1t", [128, 4160], BF16)
    ekt = sb("ekt", [128, 4112], BF16)
    cacc = sb("cacc", [128, 4112], F32)
    dgrows = sb("dgrows", [16, 512], F32)
    dcols = sb("dcols", [128, 32], F32)
    lnb = sb("lnb", [128, 32], F32)
    smax16 = sb("smax16", [128, 16], F32)
    rcb = sb("rcb", [128, 1024], BF16)
    rmcols = sb("rmcols", [128, 16], F32)
    rmT = sb("rmT", [16, 512], F32)
    t16 = sb("t16", [16, 512], F32)
    eps16 = sb("eps16", [16, 512], BF16)
    ctxden = sb("ctxden", [128, 2080], BF16)
    csxt = sb("csxt", [16, 1040], BF16)
    maskt = sb("maskt", [16, 1040], BF16)
    eqt = sb("eqt", [128, 2048], BF16)
    attnst = sb("attnst", [64, 4096], BF16)
    stg = sb("stg", [1, 64], F32)
    vcol = sb("vcol", [128, 4], F32)

    ps_a = ps("ps_a", [128, 512])
    ps_b = ps("ps_b", [128, 512])
    ps_c = ps("ps_c", [128, 512])
    ps_d = ps("ps_d", [128, 512])
    ps_e = ps("ps_e", [128, 512])
    ps_f = ps("ps_f", [128, 512])
    ps_g = ps("ps_g", [128, 512])
    ps_h = ps("ps_h", [128, 512])
    BK = {id(b): nm for b, nm in [(ps_a, "a"), (ps_b, "b"), (ps_c, "c"), (ps_d, "d"),
                                  (ps_e, "e"), (ps_f, "f"), (ps_g, "g"), (ps_h, "h")]}

    def wchunk(wt, oc, kc, width=128):
        base = (oc * 8 + kc) * 128
        return wt[:, base:base + width]

    def wv_moving(kc, half):
        v = wv[:].rearrange("p (o k c) -> p o k c", o=8, k=8)
        return v[:, half * 4:(half + 1) * 4, kc, :]

    P.op("sync", lambda s: s.dma_start(out=ones_f[:], in_=ones_f_h[:, :]), inc="ld")
    P.op("sync", lambda s: s.dma_start(out=ones_b[:], in_=ones_b_h[:, :]), inc="ld")
    P.op("sync", lambda s: s.dma_start(out=idn_f[:], in_=idn_f_h[:, :]), inc="ld")
    P.op("sync", lambda s: s.dma_start(out=ck[:], in_=ck_h[:, :]), inc="ld")
    P.op("sync", lambda s: s.dma_start(out=maskt[:], in_=maskt_h[:, :]), inc="ld")
    P.op("sync", lambda s: s.dma_start(out=sel16[:], in_=sel16_h[:, :]), inc="ld")
    P.op("sync", lambda s: s.dma_start(out=selh[:], in_=selh_h[:, :]), inc="ld")
    P.op("sync", lambda s: s.dma_start(out=bcol[:], in_=bcol_h[:, :]), inc="ld")
    P.op("sync", lambda s: s.dma_start(out=scrow[:, 0:8], in_=sc_h[:, :]), inc="ld")
    bvbm = P.op("gpsimd", lambda g: g.dma_start(out=bvb[:], in_=bvb_h[:, :]), inc="lg")
    wqm = P.op("gpsimd", lambda g: g.dma_start(out=wq[:], in_=wq_h[:, :]), inc="lg")
    wkm = P.op("gpsimd", lambda g: g.dma_start(out=wk[:], in_=wk_h[:, :]), inc="lg")
    wvm = P.op("gpsimd", lambda g: g.dma_start(out=wv[:], in_=wv_h[:, :]), inc="lg")
    wom = P.op("gpsimd", lambda g: g.dma_start(out=wo[:], in_=wo_h[:, :]), inc="lg")
    w8ld = []
    for half in range(2):
        w = P.op("gpsimd", lambda g, _h=half: g.dma_start(
            out=bufA[:], in_=w8_h[:, _h * 4096:(_h + 1) * 4096]),
            waits=[("v", w8ld[0])] if half == 1 else (), inc="lg")
        c = P.op("vector", lambda v, _h=half: v.tensor_copy(
            w8[:, _h * 4096:(_h + 1) * 4096], bufA[:]), waits=[("lg", w)], inc="v")
        w8ld.append(c)
    wp = P.op("sync", lambda s: s.dma_start(out=t1sc[:, 0:256], in_=proj_h[:, :]), inc="ld")
    proj_ready = P.op("scalar", lambda h: h.mul(proj2b[:], t1sc[:, 0:256], DSC),
                      waits=[("ld", wp)], inc="s")
    consts_ld = P.cnt["ld"]

    def _preload(h):
        from concourse.hw_specs import get_activation_tables
        tabs = get_activation_tables(nc.m.arch)
        sid = None
        for idx, funcs in enumerate(tabs.values()):
            if AF.Exp in funcs and AF.Ln in funcs:
                sid = idx
                break
        assert sid is not None
        return h.add_instruction(mybir.InstLoadActFuncSet(
            name=nc.get_next_instruction_name(), act_func_set_id=sid, ins=[], outs=[]))
    P.op("scalar", _preload)

    sc_ok = P.op("vector", lambda v: v.reciprocal(scrow[:, 16:24], scrow[:, 0:8]),
                 waits=[("ld", consts_ld)], inc="v")
    sc_ok = P.op("vector", lambda v: v.tensor_scalar(
        out=scrow[:, 8:16], in0=scrow[:, 0:8], scalar1=scrow[0:1, 1:2], scalar2=0.5,
        op0=ALU.mult, op1=ALU.mult), inc="v")
    t_b = P.op("tensor", lambda t: t.matmul(ps_a[:, 0:16], ones_f[0:1, 0:128],
                                            scrow[0:1, 0:16], start=True, stop=True),
               waits=[("v", sc_ok)], inc="t")
    sc_col = P.op("vector", lambda v: v.tensor_copy(smallcol[:, 0:16], ps_a[:, 0:16]),
                  waits=[("t", t_b)], inc="v")
    hbf_ok = P.op("vector", lambda v: v.tensor_scalar_mul(hbf[:], bcol[:, 32:40], 0.5),
                  waits=[("ld", consts_ld)], inc="v")
    P.op("vector", lambda v: v.memset(cacc[:], 0.0), inc="v")
    P.op("vector", lambda v: v.memset(smax16[:], -1e30), inc="v")
    P.op("vector", lambda v: v.memset(
        v1t[:].rearrange("p (s h c) -> p s h c", s=4, h=16)[:, :, :, 64:65], 1.0), inc="v")
    P.op("vector", lambda v: v.memset(
        ekt[:].rearrange("p (h c) -> p h c", h=16)[:, :, 256:257], 1.0), inc="v")
    setup_v = P.cnt["v"]
    P.bset("bank_a", ("v", sc_col))

    def ln_stats(buf, want_rinvs, extra_waits=(), src_bf16=False):
        ew = list(extra_waits)
        t_sx = None
        mmq = {}
        if src_bf16:
            def sxmm(t):
                last = None
                for e in range(8):
                    last = t.matmul(ps_a[0:1, :], ones_b[:, 0:1], buf[:, e * 512:(e + 1) * 512],
                                    start=(e == 0), stop=(e == 7))
                return last
            t_sx = P.op("tensor", sxmm,
                        waits=P.bw("bank_a") + [("v", P.cnt["v"]), ("s", P.cnt["s"])] + ew, inc="t")
        else:
            for e in range(8):
                hf = e % 4
                cm = P.op("vector", lambda v, _e=e, _hf=hf: v.tensor_copy(
                    eqt[:, _hf * 512:(_hf + 1) * 512], buf[:, _e * 512:(_e + 1) * 512]),
                    waits=([("t", mmq[e - 4])] if e >= 4 else P.bw("eqt")) + ew, inc="v")
                t_sx = P.op("tensor", lambda t, _e=e, _hf=hf: t.matmul(
                    ps_a[0:1, :], ones_b[:, 0:1], eqt[:, _hf * 512:(_hf + 1) * 512],
                    start=(_e == 0), stop=(_e == 7)),
                    waits=[("v", cm)] + (P.bw("bank_a") if e == 0 else []), inc="t")
                mmq[e] = t_sx
        t_sq = None
        for e in range(8):
            hf = e % 2
            s_sq = P.op("scalar", lambda h, _e=e, _hf=hf: h.activation(
                out=sqch[:, _hf * 512:(_hf + 1) * 512], in_=buf[:, _e * 512:(_e + 1) * 512],
                func=AF.Square, bias=0.0, scale=1.0),
                waits=([("t", mmq.get(100 + e - 2, P.cnt["t"]))] if e >= 2 else P.bw("sqch")) + ew,
                inc="s")
            t_sq = P.op("tensor", lambda t, _e=e, _hf=hf: t.matmul(
                ps_b[0:1, :], ones_b[:, 0:1], sqch[:, _hf * 512:(_hf + 1) * 512],
                start=(_e == 0), stop=(_e == 7)),
                waits=[("s", s_sq)] + (P.bw("bank_b") if e == 0 else []), inc="t")
            mmq[100 + e] = t_sq
        if not src_bf16:
            P.bset("eqt", ("t", P.cnt["t"]))
        P.bset("sqch", ("t", P.cnt["t"]))
        v1 = P.op("vector", lambda v: v.tensor_scalar_mul(rows[:, 0:512], ps_a[0:1, :], 1.0 / E),
                  waits=[("t", t_sx)], inc="v")
        v1b = P.op("vector", lambda v: v.tensor_scalar_mul(rows_b[:, 0:512], ps_a[0:1, :], 1.0 / E),
                   inc="v")
        P.bset("bank_a", ("v", v1b))
        v2 = P.op("vector", lambda v: v.tensor_scalar_mul(rows[:, 512:1024], ps_b[0:1, :], 1.0 / E),
                  waits=[("t", t_sq)], inc="v")
        P.bset("bank_b", ("v", v2))
        v3 = P.op("vector", lambda v: v.tensor_mul(rows[:, 1024:1536], rows[:, 0:512], rows[:, 0:512]),
                  inc="v")
        v4 = P.op("vector", lambda v: v.scalar_tensor_tensor(
            out=rows[:, 1024:1536], in0=rows[:, 512:1024], scalar=LNEPS,
            in1=rows[:, 1024:1536], op0=ALU.add, op1=ALU.subtract), inc="v")
        s1 = P.op("scalar", lambda h: h.activation(out=rows[:, 512:1024], in_=rows[:, 1024:1536],
                                                   func=AF.Ln, bias=0.0, scale=1.0),
                  waits=[("v", v4)], inc="s")
        s2 = P.op("scalar", lambda h: h.activation(out=rows[:, 1536:2048], in_=rows[:, 512:1024],
                                                   func=AF.Exp, bias=0.0, scale=-0.5),
                  inc="s")
        if want_rinvs:
            vr = P.op("vector", lambda v: v.tensor_scalar_mul(
                rows[:, 1024:1536], rows[:, 1536:2048], scrow[0:1, 16:17]),
                waits=[("s", s2)], inc="v")
            return vr, s2
        return None, s2

    def bcast_b(row_ap, bank, bankkey, n=512, wait_marks=()):
        return P.op("tensor", lambda t: t.matmul(bank[:, 0:n], ones_b[0:1, 0:128], row_ap,
                                                 start=True, stop=True),
                    waits=list(wait_marks) + P.bw(bankkey), inc="t")

    def bcast_f(row_ap, bank, bankkey, n=512, wait_marks=()):
        return P.op("tensor", lambda t: t.matmul(bank[:, 0:n], ones_f[0:1, 0:128], row_ap,
                                                 start=True, stop=True),
                    waits=list(wait_marks) + P.bw(bankkey), inc="t")

    st_x1 = {}
    st_q = {}
    prev_x1_store = None
    xload = {}

    def emit_lx(_it, first):
        lp = None
        for pc in range(4):
            lp = P.op("sync", lambda s, _pc=pc: s.dma_start(
                out=bufA[:, _pc * 1024:(_pc + 1) * 1024].rearrange("p (e t) -> p e t", e=2),
                in_=xT.rearrange("(e p) t -> p e t", p=128)[
                    :, 2 * _pc:2 * _pc + 2, _it * TS:(_it + 1) * TS]),
                waits=(([("v", setup_v)] if first else []) + P.bw("bufAx")) if pc == 0 else [],
                inc="ld")
        xload[_it] = lp
    emit_lx(0, True)
    for it in range(NT):
        lx = xload[it]
        vr, _ = ln_stats(bufA, want_rinvs=True, extra_waits=[("ld", lx)])
        bm = bcast_b(rows_b[0:1, 0:512], ps_b, "bank_b", wait_marks=[("v", P.cnt["v"])])
        bm_ = bm
        br = bcast_f(rows[0:1, 1024:1536], ps_a, "bank_a", wait_marks=[("v", vr)])
        qm = None
        qmark = {}
        for e in range(8):
            hf = e % 2
            P.op("vector", lambda v, _e=e, _hf=hf: v.scalar_tensor_tensor(
                out=t1sc[:, _hf * 512:(_hf + 1) * 512], in0=bufA[:, _e * 512:(_e + 1) * 512],
                scalar=1.0, in1=ps_b[:, :], op0=ALU.mult, op1=ALU.subtract),
                waits=[("t", bm)] + P.bw(f"t1sc{hf}"), inc="v")
            qm = P.op("vector", lambda v, _e=e, _hf=hf: v.tensor_mul(
                q8t[:, _e * 512:(_e + 1) * 512], t1sc[:, _hf * 512:(_hf + 1) * 512], ps_a[:, :]),
                waits=[("t", br)], inc="v")
            P.bset(f"t1sc{hf}", ("v", qm))
            qmark[e] = qm
        quant_done = qm
        P.bset("bank_a", ("v", quant_done))
        P.bset("bank_b", ("v", quant_done))
        for oc in range(8):
            bank = (ps_c, ps_d, ps_e, ps_f)[oc % 4]
            bkey = "bank_" + BK[id(bank)]
            tm = None
            for kp in range(4):
                tm = P.op("tensor", lambda t, _oc=oc, _kp=kp, _bank=bank: t.matmul(
                    _bank[:, :],
                    wchunk(w8, _oc, 2 * _kp, width=256).rearrange("p (k c) -> p k c", k=2),
                    q8t[:, _kp * 1024:(_kp + 1) * 1024].rearrange("p (k n) -> p k n", k=2),
                    start=(_kp == 0), stop=(_kp == 3),
                    perf_mode=mybir.MatmulPerfMode.DoubleRow),
                    waits=([("v", qmark[2 * kp + 1])] if oc == 0 else []) +
                          (P.bw(bkey) if kp == 0 else []),
                    inc="t" if kp == 3 else None)
            sm = P.op("scalar", lambda h, _oc=oc, _bank=bank: h.activation(
                out=t1sc[:, (_oc % 2) * 512:(_oc % 2 + 1) * 512], in_=_bank[:, :], func=AF.Identity,
                bias=hbf[:, _oc:_oc + 1], scale=smallcol[:, 8:9]),
                waits=[("t", tm)] + P.bw(f"t1sc{oc % 2}"), inc="s")
            P.bset(bkey, ("s", sm))
            va = P.op("vector", lambda v, _oc=oc: v.tensor_add(
                bufB[:, _oc * 512:(_oc + 1) * 512], bufA[:, _oc * 512:(_oc + 1) * 512],
                t1sc[:, (_oc % 2) * 512:(_oc % 2 + 1) * 512]),
                waits=[("s", sm)] + ([("st", prev_x1_store)] if (oc == 0 and prev_x1_store) else []),
                inc="v")
            P.bset(f"t1sc{oc % 2}", ("v", va))
        x1_done = P.cnt["v"]
        P.bset("bufAx", ("v", x1_done), ("t", P.cnt["t"]), ("s", P.cnt["s"]))
        st1 = None
        for pc in range(4):
            st1 = P.op("sync", lambda s, _it=it, _pc=pc: s.dma_start(
                out=x1_hbm.rearrange("(e p) t -> p e t", p=128)[
                    :, 2 * _pc:2 * _pc + 2, _it * TS:(_it + 1) * TS],
                in_=bufB[:, _pc * 1024:(_pc + 1) * 1024].rearrange("p (e t) -> p e t", e=2)),
                waits=[("v", x1_done)] if pc == 0 else [], inc="st")
        st_x1[it] = st1
        prev_x1_store = st1
        if it + 1 < NT:
            emit_lx(it + 1, False)
        vr2, s_r2 = ln_stats(bufB, want_rinvs=False)
        bm2 = bcast_b(rows_b[0:1, 0:512], ps_b, "bank_b", wait_marks=[("v", P.cnt["v"])])
        br2 = bcast_f(rows[0:1, 1536:2048], ps_a, "bank_a", wait_marks=[("s", s_r2)])
        ym = None
        ymark = {}
        for e in range(8):
            hf = e % 2
            P.op("vector", lambda v, _e=e, _hf=hf: v.scalar_tensor_tensor(
                out=t1sc[:, _hf * 512:(_hf + 1) * 512], in0=bufB[:, _e * 512:(_e + 1) * 512],
                scalar=1.0, in1=ps_b[:, :], op0=ALU.mult, op1=ALU.subtract),
                waits=[("t", bm2)] + P.bw(f"t1sc{hf}"), inc="v")
            ym = P.op("vector", lambda v, _e=e, _hf=hf: v.tensor_mul(
                bufD[:, _e * 512:(_e + 1) * 512], t1sc[:, _hf * 512:(_hf + 1) * 512], ps_a[:, :]),
                waits=[("t", br2)] + (P.bw("bufD") if e == 0 else []), inc="v")
            P.bset(f"t1sc{hf}", ("v", ym))
            ymark[e] = ym
        y2_done = ym
        P.bset("bank_a", ("v", y2_done))
        P.bset("bank_b", ("v", y2_done))
        for oc in range(8):
            bank = (ps_c, ps_d, ps_e, ps_f)[oc % 4]
            bkey = "bank_" + BK[id(bank)]

            tq = None
            for kc in range(8):
                tq = P.op("tensor", lambda t, _oc=oc, _kc=kc, _bank=bank: t.matmul(
                    _bank[:, :], wchunk(wq, _oc, _kc), bufD[:, _kc * 512:(_kc + 1) * 512],
                    start=(_kc == 0), stop=(_kc == 7)),
                    waits=([("v", ymark[kc]), ("lg", wqm)] if oc == 0 else []) +
                          (P.bw(bkey) if kc == 0 else []),
                    inc="t" if kc == 7 else None)
            sq_ = P.op("scalar", lambda h, _oc=oc, _bank=bank: h.activation(
                out=eqt[:, (_oc % 2) * 512:(_oc % 2 + 1) * 512], in_=_bank[:, :], func=AF.Identity,
                bias=bcol[:, _oc:_oc + 1], scale=1.0),
                waits=[("t", tq)] + P.bw(f"qch{oc % 2}") + (P.bw("eqt") if oc < 2 else []), inc="s")
            stq = P.op("sync", lambda s, _it=it, _oc=oc: s.dma_start(
                out=q_hbm.rearrange("(o p) t -> p o t", p=128)[:, _oc, _it * TS:(_it + 1) * TS],
                in_=eqt[:, (_oc % 2) * 512:(_oc % 2 + 1) * 512]),
                waits=[("s", sq_)], inc="st")
            st_q[(it, oc)] = stq
            P.bset(f"qch{oc % 2}", ("st", stq))

            def k_mm(t, _oc=oc, _bank=bank):
                last = None
                for kc in range(8):
                    last = t.matmul(_bank[:, :], wchunk(wk, _oc, kc), bufD[:, kc * 512:(kc + 1) * 512],
                                    start=(kc == 0), stop=(kc == 7))
                return last
            tk = P.op("tensor", k_mm,
                      waits=[("s", sq_)] + ([("lg", wkm)] if oc == 0 else []) +
                            (P.bw("kt") if oc == 0 else []), inc="t")
            ssq = P.op("scalar", lambda h, _oc=oc, _bank=bank: h.activation(
                out=sqch[:, (_oc % 2) * 512:(_oc % 2 + 1) * 512], in_=_bank[:, :],
                func=AF.Square, bias=bcol[:, 8 + _oc:9 + _oc], scale=1.0),
                waits=[("t", tk)] + (P.bw("sqch") if oc < 2 else [("t", P.cnt["t"])]), inc="s")
            sk_ = P.op("scalar", lambda h, _oc=oc, _bank=bank: h.activation(
                out=kt[:, _oc * 512:(_oc + 1) * 512], in_=_bank[:, :], func=AF.Identity,
                bias=bcol[:, 8 + _oc:9 + _oc], scale=1.0), inc="s")
            P.bset(bkey, ("s", sk_))
            P.op("tensor", lambda t, _oc=oc: t.matmul(
                ps_h[0:16, :], sel16[:, _oc * 16:(_oc + 1) * 16],
                sqch[:, (_oc % 2) * 512:(_oc % 2 + 1) * 512],
                start=(_oc == 0), stop=(_oc == 7)),
                waits=[("s", ssq)] + (P.bw("bank_h") if oc == 0 else []), inc="t")
        P.bset("sqch", ("t", P.cnt["t"]))
        dg_v = P.op("vector", lambda v: v.tensor_copy(dgrows[:, :], ps_h[0:16, :]),
                    waits=[("t", P.cnt["t"])], inc="v")
        P.bset("bank_h", ("v", dg_v))
        vdone = {}
        for sub in range(NSUB):
            for half in range(2):
                bank = (ps_c, ps_d, ps_e, ps_f)[(sub * 2 + half) % 4]
                bkey = "bank_" + BK[id(bank)]

                def v_mm(t, _sub=sub, _hf=half, _bank=bank):
                    last = None
                    for kc in range(8):
                        last = t.matmul(_bank[:, :],
                                        bufD[:, kc * 512 + _sub * 128: kc * 512 + (_sub + 1) * 128],
                                        wv_moving(kc, _hf), start=(kc == 0), stop=(kc == 7))
                    return last
                tv = P.op("tensor", v_mm,
                          waits=([("lg", wvm)] if (sub == 0 and half == 0) else []) + P.bw(bkey) +
                                (P.bw("v1t") if (sub == 0 and half == 0) else []), inc="t")
                ve = P.op("vector", lambda v, _sub=sub, _hf=half, _bank=bank: v.scalar_tensor_tensor(
                    out=v1t[:].rearrange("p (s h c) -> p s h c", s=4, h=16)[
                        :, _sub, _hf * 8:(_hf + 1) * 8, 0:64],
                    in0=_bank[:].rearrange("p (h c) -> p h c", h=8),
                    scalar=1.0, in1=bvb[:, _hf * 512:(_hf + 1) * 512].rearrange(
                        "p (h c) -> p h c", h=8),
                    op0=ALU.mult, op1=ALU.add),
                    waits=[("t", tv)] + ([("lg", bvbm)] if (sub == 0 and half == 0) else []), inc="v")
                P.bset(bkey, ("v", ve))
                vdone[(sub, half)] = ve
        ddbanks = [(ps_e, ps_f), (ps_c, ps_d)]

        def emit_dctr(_sub):
            _par = _sub % 2
            _dc = _par * 16
            tt = P.op("tensor", lambda t: t.transpose(
                ps_b[:, 0:16], dgrows[:, _sub * 128:(_sub + 1) * 128], idn_f[0:16, 0:16]),
                waits=[("v", dg_v)] + P.bw("bank_b"), inc="t")
            dcv = P.op("vector", lambda v: v.tensor_copy(
                dcols[:, _dc:_dc + 16], ps_b[:, 0:16]),
                waits=[("t", tt)] + P.bw(f"dcols{_par}"), inc="v")
            P.bset("bank_b", ("v", dcv))
            dctr[_sub] = dcv
        dctr = {}
        emit_dctr(0)
        for sub in range(NSUB):
            par = sub % 2
            eko = 0
            dco = par * 16
            dc_v = dctr[sub]
            expm = {}
            last_c = None

            def emit_c(h_, _sub=sub, _eko=eko):
                nonlocal last_c
                bank = (ps_g, ps_h, ps_a)[h_ % 3]
                bkey = "bank_" + BK[id(bank)]
                tc_ = P.op("tensor", lambda t, _h=h_, _s=_sub, _e=_eko, _bank=bank: t.matmul(
                    _bank[0:65, 0:257], v1t[:, _s * 1040 + _h * 65:_s * 1040 + (_h + 1) * 65],
                    ekt[:, _e + _h * 257:_e + (_h + 1) * 257], start=True, stop=True),
                    waits=[("s", expm[h_]), ("v", vdone[(_sub, 1)])] + P.bw(bkey), inc="t")
                fl = P.op("vector", lambda v, _h=h_, _bank=bank: v.tensor_add(
                    cacc[0:65, _h * 257:(_h + 1) * 257], cacc[0:65, _h * 257:(_h + 1) * 257],
                    _bank[0:65, 0:257]), waits=[("t", tc_)], inc="v")
                P.bset(bkey, ("v", fl))
                last_c = tc_

            for hp in range(8):
                bA, bB = ddbanks[hp % 2]
                kA, kB = "bank_" + BK[id(bA)], "bank_" + BK[id(bB)]
                hA, hB = 2 * hp, 2 * hp + 1
                base = hp * 512 + sub * 128
                tA = P.op("tensor", lambda t, _b=base, _bA=bA: t.matmul(
                    _bA[:, 0:256], kt[0:64, _b:_b + 128], proj2b[0:64, :],
                    start=True, stop=True),
                    waits=P.bw(kA) + ([("v", dc_v)] if hp == 0 else []), inc="t")
                tB = P.op("tensor", lambda t, _b=base, _bB=bB: t.matmul(
                    _bB[:, 0:256], kt[64:128, _b:_b + 128], proj2b[64:128, :],
                    start=True, stop=True, tile_position=(64, 0)),
                    waits=P.bw(kB), inc="t")
                ekw = P.bw("ekt") if hp == 0 else []
                eA = P.op("scalar", lambda h, _hA=hA, _bA=bA, _e=eko, _dc=dco: h.activation(
                    out=ekt[:, _e + _hA * 257:_e + _hA * 257 + 256], in_=_bA[:, 0:256], func=AF.Exp,
                    bias=dcols[:, _dc + _hA:_dc + _hA + 1], scale=1.0),
                    waits=[("t", tA)] + ekw, inc="s")
                P.bset(kA, ("s", eA))
                eB = P.op("scalar", lambda h, _hB=hB, _bB=bB, _e=eko, _dc=dco: h.activation(
                    out=ekt[:, _e + _hB * 257:_e + _hB * 257 + 256], in_=_bB[:, 0:256], func=AF.Exp,
                    bias=dcols[:, _dc + _hB:_dc + _hB + 1], scale=1.0),
                    waits=[("t", tB)], inc="s")
                P.bset(kB, ("s", eB))
                expm[hA] = eA
                expm[hB] = eB
                if hp == 4:
                    vr1 = P.op("vector", lambda v, _dc=dco: v.reduce_max(
                        lnb[:, _dc:_dc + 8].rearrange("p (h o) -> p h o", h=8),
                        ekt[:, 0:2056].rearrange("p (h c) -> p h c", h=8)[:, :, 0:256],
                        axis=mybir.AxisListType.X), waits=[("s", expm[7])], inc="v")
                if hp >= 1:
                    emit_c(2 * (hp - 1))
                    emit_c(2 * (hp - 1) + 1)
            ek_done = P.cnt["s"]
            if sub + 1 < NSUB:
                emit_dctr(sub + 1)
            vr2_ = P.op("vector", lambda v, _dc=dco: v.reduce_max(
                lnb[:, _dc + 8:_dc + 16].rearrange("p (h o) -> p h o", h=8),
                ekt[:, 2056:4112].rearrange("p (h c) -> p h c", h=8)[:, :, 0:256],
                axis=mybir.AxisListType.X), waits=[("s", ek_done)], inc="v")
            v_rm = vr2_
            s_ln = P.op("scalar", lambda h, _dc=dco: h.activation(
                out=lnb[:, _dc:_dc + 16], in_=lnb[:, _dc:_dc + 16],
                func=AF.Ln, bias=0.0, scale=1.0),
                waits=[("v", v_rm)], inc="s")
            v_sb = P.op("vector", lambda v, _dc=dco: v.tensor_sub(
                lnb[:, _dc:_dc + 16], lnb[:, _dc:_dc + 16], dcols[:, _dc:_dc + 16]),
                waits=[("s", s_ln)], inc="v")
            vsm = P.op("vector", lambda v, _dc=dco: v.tensor_max(
                smax16[:, :], smax16[:, :], lnb[:, _dc:_dc + 16]), inc="v")
            P.bset(f"dcols{par}", ("v", vsm))
            emit_c(14)
            emit_c(15)
            P.bset("ekt", ("t", last_c), ("v", v_rm))
        P.bset("v1t", ("t", P.cnt["t"]))
        P.bset("kt", ("t", P.cnt["t"]))
        P.bset("bufD", ("t", P.cnt["t"]))
    v_f1 = P.op("vector", lambda v: v.reduce_max(vcol[:, 0:1], smax16[:, :],
                                                 axis=mybir.AxisListType.X), inc="v")
    t_f1 = P.op("tensor", lambda t: t.transpose(ps_g[0:1, 0:128], vcol[:, 0:1], idn_f[:, :]),
                waits=[("v", v_f1)] + P.bw("bank_g"), inc="t")
    v_f2 = P.op("vector", lambda v: v.memset(stg[:, :], 0.0), inc="v")
    v_f3 = P.op("vector", lambda v: v.reduce_max(stg[0:1, 0:1], ps_g[0:1, 0:128],
                                                 axis=mybir.AxisListType.X),
                waits=[("t", t_f1), ("v", v_f2)], inc="v")
    P.bset("bank_g", ("v", v_f3))

    prev_out_store = None
    cload = {}

    def emit_lq(_it):
        lq = None
        for pc in range(4):
            lq = P.op("gpsimd", lambda g, _pc=pc: g.dma_start(
                out=kt[:, _pc * 1024:(_pc + 1) * 1024].rearrange("p (o t) -> p o t", o=2),
                in_=q_hbm.rearrange("(o p) t -> p o t", p=128)[
                    :, 2 * _pc:2 * _pc + 2, _it * TS:(_it + 1) * TS]),
                waits=(([("st", st_q[(NT - 1, 7)])] if _it == 0 else []) + P.bw("kt"))
                      if pc == 0 else [],
                inc="lg")
        cload[_it] = lq

    def emit_lx1(_it, extra):
        lx1 = None
        for pc in range(4):
            lx1 = P.op("gpsimd", lambda g, _pc=pc: g.dma_start(
                out=bufB[:, _pc * 1024:(_pc + 1) * 1024].rearrange("p (e t) -> p e t", e=2),
                in_=x1_hbm.rearrange("(e p) t -> p e t", p=128)[
                    :, 2 * _pc:2 * _pc + 2, _it * TS:(_it + 1) * TS]),
                waits=(list(extra) + [("st", st_x1[_it])]) if pc == 0 else [], inc="lg")
        cload[(_it, "x1")] = lx1

    d2 = P.op("gpsimd", lambda g: g.dma_start(out=stab_in[0:1, 0:64], in_=stg[:, :]),
              waits=[("v", v_f3)], inc="lg")
    d1a = P.op("gpsimd", lambda g: g.dma_start(out=p_in_a[:, :], in_=cacc[0:65, 0:2056]),
               waits=[("v", P.cnt["v"])], inc="lg")
    d1b = P.op("gpsimd", lambda g: g.dma_start(out=p_in_b[:, :], in_=cacc[0:65, 2056:4112]),
               inc="lg")
    emit_lq(0)
    emit_lx1(0, [])
    cc2 = P.op("gpsimd", lambda g: g.collective_compute(
        "AllReduce", ALU.max, replica_groups=[[0, 1, 2, 3, 4, 5, 6, 7]],
        ins=[stab_in.ap().opt()], outs=[stab_out.ap().opt()]), waits=[("lg", d2)], inc="cc")
    cc1a = P.op("gpsimd", lambda g: g.collective_compute(
        "AllReduce", ALU.add, replica_groups=[[0, 1, 2, 3], [4, 5, 6, 7]],
        ins=[p_in_a.ap().opt()], outs=[p_out_a.ap().opt()]), waits=[("lg", d1a)], inc="cc")
    cc1b = P.op("gpsimd", lambda g: g.collective_compute(
        "AllReduce", ALU.add, replica_groups=[[0, 1, 2, 3], [4, 5, 6, 7]],
        ins=[p_in_b.ap().opt()], outs=[p_out_b.ap().opt()]), waits=[("lg", d1b)], inc="cc")
    d4 = P.op("gpsimd", lambda g: g.dma_start(out=stg[:, :], in_=stab_out[0:1, 0:64]),
              waits=[("cc", cc2)], inc="lg")
    d3a = P.op("gpsimd", lambda g: g.dma_start(out=cacc[0:65, 0:2056], in_=p_out_a[:, :]),
               waits=[("cc", cc1a)], inc="lg")
    d3b = P.op("gpsimd", lambda g: g.dma_start(out=cacc[0:65, 2056:4112], in_=p_out_b[:, :]),
               waits=[("cc", cc1b)], inc="lg")
    _asm = {}

    def emit_assembly():
        s_s1 = P.op("scalar", lambda h: h.activation(out=scrow[:, 24:32], in_=stg[0:1, 0:8],
                                                     func=AF.Exp, bias=0.0, scale=-1.0),
                    waits=[("lg", d4)], inc="s")
        t_s1 = P.op("tensor", lambda t: t.matmul(ps_a[:, 0:4], ones_f[0:1, 0:128], scrow[0:1, 24:28],
                                                 start=True, stop=True),
                    waits=[("s", s_s1)] + P.bw("bank_a"), inc="t")
        v_s1 = P.op("vector", lambda v: v.tensor_copy(smallcol[:, 12:16], ps_a[:, 0:4]),
                    waits=[("t", t_s1)], inc="v")
        P.bset("bank_a", ("v", v_s1))
        P.op("sync", lambda s: s.dma_start(out=dbg[0:65, :], in_=cacc[0:65, :]),
             waits=[("lg", d3b)], inc="st")
        csx_t = None
        for h_ in range(16):
            dwait = d3a if h_ < 8 else d3b
            tv1 = P.op("tensor", lambda t, _h=h_: t.transpose(
                ps_b[0:1, 0:65], cacc[0:65, _h * 257 + 256:_h * 257 + 257], idn_f[0:65, 0:65]),
                waits=[("lg", dwait)] + P.bw("bank_b"), inc="t")
            vv1 = P.op("vector", lambda v: v.tensor_scalar_mul(rows[:, 0:65], ps_b[0:1, 0:65], EPS),
                       waits=[("t", tv1)], inc="v")
            vv1b = P.op("vector", lambda v: v.memset(rows[:, 64:65], EPS * NTOK), inc="v")
            P.bset("bank_b", ("v", vv1b))
            tb = P.op("tensor", lambda t: t.matmul(ps_b[:, 0:65], ones_f[0:1, 0:128], rows[0:1, 0:65],
                                                   start=True, stop=True),
                      waits=[("v", vv1b)] + P.bw("bank_b"), inc="t")
            vb = P.op("vector", lambda v: v.tensor_copy(rcb[:, 0:65], ps_b[:, 0:65]),
                      waits=[("t", tb)], inc="v")
            P.bset("bank_b", ("v", vb))
            cm = []
            for c in range(2):
                ta = P.op("tensor", lambda t, _h=h_, _c=c: t.transpose(
                    ps_a[:, 0:65], cacc[0:65, _h * 257 + _c * 128:_h * 257 + (_c + 1) * 128],
                    idn_f[0:65, 0:65]), waits=P.bw("bank_a"), inc="t")
                vc_ = P.op("vector", lambda v, _h=h_, _c=c: v.scalar_tensor_tensor(
                    out=ctxden[:, (_h * 2 + _c) * 65:(_h * 2 + _c + 1) * 65], in0=ps_a[:, 0:65],
                    scalar=smallcol[:, 12:13], in1=rcb[:, 0:65], op0=ALU.mult, op1=ALU.add),
                    waits=[("t", ta), ("v", vb), ("v", v_s1)], inc="v")
                P.bset("bank_a", ("v", vc_))
                cm.append(vc_)
            for c in range(2):
                csx_t = P.op("tensor", lambda t, _h=h_, _c=c: t.matmul(
                    ps_h[0:16, 0:65], selh[:, _h * 16:(_h + 1) * 16],
                    ctxden[:, (_h * 2 + _c) * 65:(_h * 2 + _c + 1) * 65],
                    start=(_h == 0 and _c == 0), stop=(_h == 15 and _c == 1)),
                    waits=[("v", cm[c])] + (P.bw("bank_h") if (h_ == 0 and c == 0) else []),
                    inc="t")
        vcs = P.op("vector", lambda v: v.tensor_mul(
            csxt[:].rearrange("p (h c) -> p h c", h=16), maskt[:].rearrange("p (h c) -> p h c", h=16),
            ps_h[0:16, 0:65].unsqueeze(1).broadcast_to([16, 16, 65])),
            waits=[("t", csx_t)], inc="v")
        P.bset("bank_h", ("v", vcs))
        _asm["vcs"] = vcs


    for it in range(NT):
        lq = cload[it]
        lx1 = cload[(it, "x1")]
        for oc in range(8):
            hf = oc % 2
            if oc % 2 == 0:
                ssq = P.op("vector", lambda v, _oc=oc, _hf=hf: v.tensor_mul(
                    sqch[:, _hf * 512:(_hf + 1) * 512], kt[:, _oc * 512:(_oc + 1) * 512],
                    kt[:, _oc * 512:(_oc + 1) * 512]),
                    waits=[("lg", lq)] + (P.bw("sqch") if oc < 2 else [("t", P.cnt["t"])]), inc="v")
                sqw = ("v", ssq)
            else:
                ssq = P.op("scalar", lambda h, _oc=oc, _hf=hf: h.activation(
                    out=sqch[:, _hf * 512:(_hf + 1) * 512], in_=kt[:, _oc * 512:(_oc + 1) * 512],
                    func=AF.Square, bias=0.0, scale=1.0),
                    waits=[("lg", lq)] + (P.bw("sqch") if oc < 2 else [("t", P.cnt["t"])]), inc="s")
                sqw = ("s", ssq)
            P.op("tensor", lambda t, _oc=oc, _hf=hf: t.matmul(
                ps_h[0:16, :], sel16[:, _oc * 16:(_oc + 1) * 16],
                sqch[:, _hf * 512:(_hf + 1) * 512],
                start=(_oc == 0), stop=(_oc == 7)),
                waits=[sqw] + (P.bw("bank_h") if oc == 0 else []), inc="t")
        P.bset("sqch", ("t", P.cnt["t"]))
        dg_v = P.op("vector", lambda v: v.tensor_copy(dgrows[:, :], ps_h[0:16, :]),
                    waits=[("t", P.cnt["t"])] + P.bw("dgrows"), inc="v")
        P.bset("bank_h", ("v", dg_v))
        rmbanks = [(ps_e, ps_f), (ps_c, ps_d)]
        for sub in range(NSUB):
            for hp in range(8):
                bA, bB = rmbanks[hp % 2]
                kA, kB = "bank_" + BK[id(bA)], "bank_" + BK[id(bB)]
                hA, hB = 2 * hp, 2 * hp + 1
                base = hp * 512 + sub * 128
                tA = P.op("tensor", lambda t, _b=base, _bA=bA: t.matmul(
                    _bA[:, 0:256], kt[0:64, _b:_b + 128], proj2b[0:64, :],
                    start=True, stop=True), waits=P.bw(kA), inc="t")
                tB = P.op("tensor", lambda t, _b=base, _bB=bB: t.matmul(
                    _bB[:, 0:256], kt[64:128, _b:_b + 128], proj2b[64:128, :],
                    start=True, stop=True, tile_position=(64, 0)), waits=P.bw(kB), inc="t")
                rA = P.op("vector", lambda v, _hA=hA, _bA=bA: v.reduce_max(
                    rmcols[:, _hA:_hA + 1], _bA[:, 0:256], axis=mybir.AxisListType.X),
                    waits=[("t", tA)] + (P.bw("rmcols") if (sub == 0 and hp == 0) else []), inc="v")
                P.bset(kA, ("v", rA))
                rB = P.op("vector", lambda v, _hB=hB, _bB=bB: v.reduce_max(
                    rmcols[:, _hB:_hB + 1], _bB[:, 0:256], axis=mybir.AxisListType.X),
                    waits=[("t", tB)], inc="v")
                P.bset(kB, ("v", rB))
            rm_v = P.cnt["v"]
            trm = P.op("tensor", lambda t: t.transpose(ps_g[0:16, 0:128], rmcols[:, :], idn_f[:, :]),
                       waits=[("v", rm_v)] + P.bw("bank_g"), inc="t")
            vtm = P.op("vector", lambda v, _sub=sub: v.tensor_copy(
                rmT[:, _sub * 128:(_sub + 1) * 128], ps_g[0:16, 0:128]),
                waits=[("t", trm)], inc="v")
            P.bset("bank_g", ("v", vtm))
        ve1 = P.op("vector", lambda v: v.scalar_tensor_tensor(
            out=t16[:, :], in0=rmT[:, :], scalar=-K0, in1=dgrows[:, :],
            op0=ALU.add, op1=ALU.subtract),
            waits=[("v", P.cnt["v"])], inc="v")
        P.bset("dgrows", ("v", ve1))
        P.bset("rmcols", ("v", ve1))
        se1 = P.op("scalar", lambda h: h.activation(out=eps16[:, :], in_=t16[:, :],
                                                    func=AF.Exp, bias=0.0, scale=1.0),
                   waits=[("v", ve1)] + P.bw("eps16"), inc="s")
        if it == 0:
            emit_assembly()
        vcs = _asm["vcs"]
        pending_ln = []
        pending_div = []
        for hp in range(8):
            while pending_div:
                pending_div.pop(0)()
            hA, hB = 2 * hp, 2 * hp + 1
            qbase = hp * 512
            banksA = (ps_c, ps_d)
            banksB = (ps_e, ps_f)
            em = []
            for c in range(2):
                kA = "bank_" + BK[id(banksA[c])]
                kB = "bank_" + BK[id(banksB[c])]
                tfa = P.op("tensor", lambda t, _c=c, _qb=qbase, _bk=banksA: t.matmul(
                    _bk[_c][:, :], proj2b[0:64, _c * 128:(_c + 1) * 128], kt[0:64, _qb:_qb + 512],
                    start=True, stop=True), waits=P.bw(kA), inc="t")
                ea = P.op("scalar", lambda h, _c=c, _bk=banksA: h.activation(
                    out=eqt[:, _c * 512:(_c + 1) * 512], in_=_bk[_c][:, :], func=AF.Exp,
                    bias=ck[:, 0:1], scale=1.0),
                    waits=[("t", tfa)] + (P.bw("eqt") if (hp == 0 and c == 0) else []), inc="s")
                P.bset(kA, ("s", ea))
                tfb = P.op("tensor", lambda t, _c=c, _qb=qbase, _bk=banksB: t.matmul(
                    _bk[_c][:, :], proj2b[64:128, _c * 128:(_c + 1) * 128], kt[64:128, _qb:_qb + 512],
                    start=True, stop=True, tile_position=(64, 0)), waits=P.bw(kB), inc="t")
                eb = P.op("scalar", lambda h, _c=c, _bk=banksB: h.activation(
                    out=eqt[:, 1024 + _c * 512:1024 + (_c + 1) * 512], in_=_bk[_c][:, :], func=AF.Exp,
                    bias=ck[:, 0:1], scale=1.0), waits=[("t", tfb)], inc="s")
                P.bset(kB, ("s", eb))
                em.append((ea, eb))
            if pending_ln:
                pending_div.append(pending_ln.pop(0)())

            def numden(_h, _eqoff, _bank, _bkey, _em, extra):
                P.op("tensor", lambda t: t.matmul(
                    _bank[0:65, :], ctxden[:, (_h * 2) * 65:(_h * 2 + 1) * 65],
                    eqt[:, _eqoff:_eqoff + 512], start=True, stop=False),
                    waits=[("s", _em[0]), ("s", se1)] + P.bw(_bkey) + extra)
                P.op("tensor", lambda t: t.matmul(
                    _bank[0:65, :], ctxden[:, (_h * 2 + 1) * 65:(_h * 2 + 2) * 65],
                    eqt[:, _eqoff + 512:_eqoff + 1024], start=False, stop=False),
                    waits=[("s", _em[1])])
                return P.op("tensor", lambda t: t.matmul(
                    _bank[0:65, :], csxt[:, _h * 65:(_h + 1) * 65],
                    eps16[:, :], start=False, stop=True), inc="t")
            tnA = numden(hA, 0, ps_g, "bank_g", [em[0][0], em[1][0]],
                         [("v", vcs)] if (it == 0 and hp == 0) else [])
            tnB = numden(hB, 1024, ps_h, "bank_h", [em[0][1], em[1][1]], [])
            slA = (hp % 2) * 1024
            slB = (hp % 2) * 1024 + 512
            cpA = P.op("vector", lambda v, _sl=slA: v.tensor_copy(
                bufA[0:65, _sl:_sl + 512], ps_g[0:65, :]),
                waits=[("t", tnA)] + P.bw(f"nsl{hp % 2}A"), inc="v")
            P.bset("bank_g", ("v", cpA))
            cpB = P.op("vector", lambda v, _sl=slB: v.tensor_copy(
                bufA[0:65, _sl:_sl + 512], ps_h[0:65, :]),
                waits=[("t", tnB)] + P.bw(f"nsl{hp % 2}B"), inc="v")
            P.bset("bank_h", ("v", cpB))
            def _emit_ln(_hp=hp, _cpB=cpB, _slA=slA, _slB=slB, _it=it):
                sl0 = min(_slA, _slB)
                par = _hp % 2
                rowbuf = sqch if par == 0 else rcb
                P.op("scalar", lambda h, _s0=sl0: h.activation(
                    out=t1sc[64:65, 0:1024], in_=bufA[64:65, _s0:_s0 + 1024], func=AF.Ln,
                    bias=0.0, scale=1.0),
                    waits=[("v", _cpB)] +
                          ((P.bw("t1sc0") + P.bw("t1sc1")) if _hp == 0 else []), inc="s")
                scAB = P.op("scalar", lambda h, _rb=rowbuf: h.activation(
                    out=_rb[64:65, 0:1024], in_=t1sc[64:65, 0:1024], func=AF.Exp, bias=0.0,
                    scale=-1.0),
                    waits=P.bw(f"drow{par}") +
                          (P.bw("sqch") if (par == 0 and _hp < 2) else []) +
                          ([("v", _asm["vcs"])] if (_it == 0 and _hp == 1) else []),
                    inc="s")

                def _emit_div(_rb=rowbuf, _par=par):
                    bA_ = P.op("tensor", lambda t: t.matmul(ps_a[0:64, :], ones_b[64:65, 0:64],
                                                            _rb[64:65, 0:512], start=True, stop=True,
                                                            tile_position=(64, 0)),
                               waits=[("s", scAB)] + P.bw("bank_a"), inc="t")
                    vrA = P.op("vector", lambda v: v.tensor_copy(rcb[0:64, 0:512], ps_a[0:64, :]),
                               waits=[("t", bA_)] + P.bw("rcbA"), inc="v")
                    P.bset("bank_a", ("v", vrA))
                    evA = P.op("vector", lambda v: v.tensor_mul(
                        bufD[0:64, _hp * 512:(_hp + 1) * 512], bufA[0:64, _slA:_slA + 512],
                        rcb[0:64, 0:512]),
                        waits=[("v", vrA)] + (P.bw("bufDc") if _hp == 0 else []), inc="v")
                    P.bset(f"nsl{_hp % 2}A", ("v", evA))
                    P.bset("rcbA", ("v", evA))
                    bB_ = P.op("tensor", lambda t: t.matmul(ps_b[0:64, :], ones_b[64:65, 0:64],
                                                            _rb[64:65, 512:1024], start=True,
                                                            stop=True, tile_position=(64, 0)),
                               waits=[("s", scAB)] + P.bw("bank_b"), inc="t")
                    P.bset(f"drow{_par}", ("t", bB_))
                    vrB = P.op("vector", lambda v: v.tensor_copy(rcb[0:64, 512:1024], ps_b[0:64, :]),
                               waits=[("t", bB_)] + P.bw("rcbB"), inc="v")
                    P.bset("bank_b", ("v", vrB))
                    evB = P.op("vector", lambda v: v.tensor_mul(
                        attnst[0:64, _hp * 512:(_hp + 1) * 512], bufA[0:64, _slB:_slB + 512],
                        rcb[0:64, 512:1024]),
                        waits=[("v", vrB)], inc="v")
                    P.bset(f"nsl{_hp % 2}B", ("v", evB))
                    P.bset("rcbB", ("v", evB))
                    mvp = P.op("sync", lambda sy: sy.dma_start(
                        out=bufD[64:128, _hp * 512:(_hp + 1) * 512],
                        in_=attnst[0:64, _hp * 512:(_hp + 1) * 512]),
                        waits=[("v", evB)] + (P.bw("bufDmv") if _hp == 0 else []), inc="st")
                    P.bset("mvlast", ("st", mvp))
                return _emit_div
            pending_ln.append(_emit_ln)
        while pending_ln:
            pending_div.append(pending_ln.pop(0)())
        while pending_div:
            pending_div.pop(0)()
        kt_free_t = P.cnt["t"]
        kt_free_s = P.cnt["s"]
        P.bset("kt", ("t", kt_free_t), ("s", kt_free_s), ("v", P.cnt["v"]))
        if it + 1 < NT:
            emit_lq(it + 1)
        P.bset("eqt", ("t", P.cnt["t"]))
        P.bset("sqch", ("t", P.cnt["t"]))
        attn_v = P.cnt["v"]
        mv = P.bw("mvlast")[0][1] if P.bw("mvlast") else None
        for oc in range(8):
            bank = (ps_c, ps_d, ps_e, ps_f)[oc % 4]
            bkey = "bank_" + BK[id(bank)]

            def wo_mm(t, _oc=oc, _bank=bank):
                last = None
                for kc in range(8):
                    last = t.matmul(_bank[:, :], wchunk(wo, _oc, kc), bufD[:, kc * 512:(kc + 1) * 512],
                                    start=(kc == 0), stop=(kc == 7))
                return last
            tw = P.op("tensor", wo_mm,
                      waits=(([("st", mv)] if mv else []) + [("lg", wom)] if oc == 0 else []) + P.bw(bkey), inc="t")
            sw_ = P.op("scalar", lambda h, _oc=oc, _bank=bank: h.activation(
                out=t1sc[:, (_oc % 2) * 512:(_oc % 2 + 1) * 512], in_=_bank[:, :], func=AF.Identity,
                bias=bcol[:, 24 + _oc:25 + _oc], scale=1.0),
                waits=[("t", tw)] + P.bw(f"t1sc{oc % 2}"), inc="s")
            P.bset(bkey, ("s", sw_))
            va = P.op("vector", lambda v, _oc=oc: v.tensor_add(
                bufB[:, _oc * 512:(_oc + 1) * 512], bufB[:, _oc * 512:(_oc + 1) * 512],
                t1sc[:, (_oc % 2) * 512:(_oc % 2 + 1) * 512]),
                waits=[("s", sw_), ("lg", lx1)] +
                      ([("st", prev_out_store)] if (oc == 0 and prev_out_store) else []), inc="v")
            P.bset(f"t1sc{oc % 2}", ("v", va))
        x2_done = P.cnt["v"]
        vr3, _ = ln_stats(bufB, want_rinvs=True)
        bm3 = bcast_b(rows_b[0:1, 0:512], ps_b, "bank_b", wait_marks=[("v", P.cnt["v"])])
        br3 = bcast_f(rows[0:1, 1024:1536], ps_a, "bank_a", wait_marks=[("v", vr3)])
        qm3 = None
        qmark3 = {}
        for e in range(8):
            hf = e % 2
            P.op("vector", lambda v, _e=e, _hf=hf: v.scalar_tensor_tensor(
                out=t1sc[:, _hf * 512:(_hf + 1) * 512], in0=bufB[:, _e * 512:(_e + 1) * 512],
                scalar=1.0, in1=ps_b[:, :], op0=ALU.mult, op1=ALU.subtract),
                waits=[("t", bm3)] + P.bw(f"t1sc{hf}"), inc="v")
            qm3 = P.op("vector", lambda v, _e=e, _hf=hf: v.tensor_mul(
                q8t[:, _e * 512:(_e + 1) * 512], t1sc[:, _hf * 512:(_hf + 1) * 512], ps_a[:, :]),
                waits=[("t", br3)], inc="v")
            P.bset(f"t1sc{hf}", ("v", qm3))
            qmark3[e] = qm3
        P.bset("bank_a", ("v", qm3))
        P.bset("bank_b", ("v", qm3))
        for oc in range(8):
            bank = (ps_c, ps_d, ps_e, ps_f)[oc % 4]
            bkey = "bank_" + BK[id(bank)]
            tm = None
            for kp in range(4):
                tm = P.op("tensor", lambda t, _oc=oc, _kp=kp, _bank=bank: t.matmul(
                    _bank[:, :],
                    wchunk(w8, _oc, 2 * _kp, width=256).rearrange("p (k c) -> p k c", k=2),
                    q8t[:, _kp * 1024:(_kp + 1) * 1024].rearrange("p (k n) -> p k n", k=2),
                    start=(_kp == 0), stop=(_kp == 3),
                    perf_mode=mybir.MatmulPerfMode.DoubleRow),
                    waits=([("v", qmark3[2 * kp + 1])] if oc == 0 else []) +
                          (P.bw(bkey) if kp == 0 else []),
                    inc="t" if kp == 3 else None)
            sm = P.op("scalar", lambda h, _oc=oc, _bank=bank: h.activation(
                out=t1sc[:, (_oc % 2) * 512:(_oc % 2 + 1) * 512], in_=_bank[:, :], func=AF.Identity,
                bias=hbf[:, _oc:_oc + 1], scale=smallcol[:, 8:9]),
                waits=[("t", tm)] + P.bw(f"t1sc{oc % 2}"), inc="s")
            P.bset(bkey, ("s", sm))
            vo = P.op("vector", lambda v, _oc=oc: v.tensor_add(
                bufB[:, _oc * 512:(_oc + 1) * 512], bufB[:, _oc * 512:(_oc + 1) * 512],
                t1sc[:, (_oc % 2) * 512:(_oc % 2 + 1) * 512]),
                waits=[("s", sm)], inc="v")
            P.bset(f"t1sc{oc % 2}", ("v", vo))
        out_done = P.cnt["v"]
        P.bset("bufDc", ("v", out_done))
        so = P.op("sync", lambda s, _it=it: s.dma_start(
            out=xoT.rearrange("(e p) t -> p e t", p=128)[:, :, _it * TS:(_it + 1) * TS],
            in_=bufB[:].rearrange("p (e t) -> p e t", e=8)),
            waits=[("v", out_done)], inc="st")
        prev_out_store = so
        if it + 1 < NT:
            emit_lx1(it + 1, [("st", so), ("v", out_done)])
        P.bset("eps16", ("t", P.cnt["t"]))
        P.bset("bufD", ("t", P.cnt["t"]))
        P.bset("bufDmv", ("t", P.cnt["t"]))
    final_st = P.cnt["st"]

    with stack:
        with (
            nc.semaphore("ld") as s_ld,
            nc.semaphore("lg") as s_lg,
            nc.semaphore("st") as s_st,
            nc.semaphore("t") as s_t,
            nc.semaphore("v") as s_v,
            nc.semaphore("s") as s_s,
            nc.semaphore("g") as s_g,
            nc.semaphore("cc") as s_cc,
            nc.Block() as block,
        ):
            P.sem = {"ld": s_ld, "lg": s_lg, "st": s_st, "t": s_t, "v": s_v,
                     "s": s_s, "g": s_g, "cc": s_cc}

            @block.sync
            def _(h):
                for fn in P.ops["sync"]:
                    fn(h)
                h.wait_ge(s_st, final_st)

            @block.tensor
            def _(h):
                for fn in P.ops["tensor"]:
                    fn(h)

            @block.vector
            def _(h):
                for fn in P.ops["vector"]:
                    fn(h)

            @block.scalar
            def _(h):
                for fn in P.ops["scalar"]:
                    fn(h)

            @block.gpsimd
            def _(h):
                for fn in P.ops["gpsimd"]:
                    fn(h)
    return nc


def _prep_weight(w):
    return np.ascontiguousarray(
        w.reshape(8, 128, 8, 128).transpose(3, 0, 2, 1).reshape(128, 8192))


def _host_inputs(L0_attn_x, Wq, bq, Wk, bk, Wv, bv, Wo, bo, proj,
                 w8, bf_, iscale, wscale):
    base = {}
    base["wq_h"] = _prep_weight(np.asarray(Wq, np.float32))
    base["wk_h"] = _prep_weight(np.asarray(Wk, np.float32))
    base["wv_h"] = _prep_weight(np.asarray(Wv, np.float32))
    base["wo_h"] = _prep_weight(np.asarray(Wo, np.float32))
    base["w8_h"] = _prep_weight(np.asarray(w8, np.float32))
    base["proj_h"] = np.ascontiguousarray(
        np.concatenate([np.asarray(proj, np.float32).T] * 2, axis=0))
    bc = np.zeros((128, 40), np.float32)
    for i, b_ in enumerate([bq, bk, bv, bo, bf_]):
        bc[:, i * 8:(i + 1) * 8] = np.asarray(b_, np.float32).reshape(8, 128).T
    base["bcol_h"] = bc
    base["bvb_h"] = np.ascontiguousarray(
        np.broadcast_to(np.asarray(bv, np.float32)[None, :], (128, 1024)))
    sc = np.ones((1, 8), np.float32)
    sc[0, 0] = float(np.asarray(iscale).reshape(-1)[0])
    sc[0, 1] = float(np.asarray(wscale).reshape(-1)[0])
    base["sc_h"] = sc
    s16 = np.zeros((128, 128), np.float32)
    for c in range(8):
        s16[0:64, c * 16 + 2 * c] = NDIAG
        s16[64:128, c * 16 + 2 * c + 1] = NDIAG
    base["sel16_h"] = s16.astype(ml_dtypes.bfloat16)
    sh = np.zeros((128, 256), np.float32)
    for h in range(16):
        sh[:, h * 16 + h] = 1.0
    base["selh_h"] = sh.astype(ml_dtypes.bfloat16)
    base["ones_f_h"] = np.ones((128, 128), np.float32)
    base["ones_b_h"] = np.ones((128, 128), np.float32).astype(ml_dtypes.bfloat16)
    base["idn_f_h"] = np.eye(128, dtype=np.float32)
    ckv = np.zeros((128, 4), np.float32)
    ckv[:, 0] = -K0
    base["ck_h"] = ckv
    mk = np.zeros((16, 1040), np.float32)
    for h in range(16):
        mk[h, h * 65:(h + 1) * 65] = EPS
    base["maskt_h"] = mk.astype(ml_dtypes.bfloat16)
    return base


def kernel(L0_attn_x, L0_attn_Wq, L0_attn_bq, L0_attn_Wk, L0_attn_bk,
           L0_attn_Wv, L0_attn_bv, L0_attn_Wo, L0_attn_bo, L0_attn_proj,
           L0_ffn_weight_fp8, L0_ffn_bias, L0_ffn_input_scale, L0_ffn_weight_scale):
    x = np.asarray(L0_attn_x, np.float32)
    base = _host_inputs(x, L0_attn_Wq, L0_attn_bq, L0_attn_Wk, L0_attn_bk,
                        L0_attn_Wv, L0_attn_bv, L0_attn_Wo, L0_attn_bo,
                        L0_attn_proj, L0_ffn_weight_fp8, L0_ffn_bias,
                        L0_ffn_input_scale, L0_ffn_weight_scale)
    nc = build_program()
    in_maps = []
    for c in range(8):
        b_, tc = c // 4, c % 4
        xs = np.ascontiguousarray(x[b_, tc * T:(tc + 1) * T, :].T)
        m = dict(base)
        m["xT"] = xs
        in_maps.append(m)
    res = run_bass_kernel_spmd(nc, in_maps, core_ids=list(range(8)))
    out = np.empty((B, N, E), np.float32)
    for c in range(8):
        b_, tc = c // 4, c % 4
        out[b_, tc * T:(tc + 1) * T, :] = res.results[c]["xoT"].T
    return out


if __name__ == "__main__":
    z = np.load("/root/problem/exp/ref_data.npz")
    ins = {k: z[k] for k in z.files if k != "expected"}
    got = kernel(**ins)
    exp = z["expected"]
    d = got - exp
    rvr = float((d ** 2).mean() / (exp ** 2).mean())
    print(f"kernel vs reference: rvr={rvr:.3e} relL2={np.sqrt(rvr):.3e}")


# version 26
# speedup vs baseline: 1.0460x; 1.0030x over previous
import numpy as np
import ml_dtypes
import concourse.bass as bass
import concourse.mybir as mybir
from concourse.bass_utils import run_bass_kernel_spmd

F32 = mybir.dt.float32
BF16 = mybir.dt.bfloat16
F8 = mybir.dt.float8e4
AF = mybir.ActivationFunctionType
ALU = mybir.AluOpType

E, H, D, M = 1024, 16, 64, 256
B, N = 2, 16384
T = 4096
TS = 512
NT = T // TS
SUB = 128
NSUB = TS // SUB
EPS = 1e-4
K0 = 20.0
LNEPS = 1e-5
DSC = float(D ** -0.25)
NDIAG = -0.5 * float(D ** -0.5)
NTOK = float(N)

ENGS = ("sync", "tensor", "vector", "scalar", "gpsimd")
SEMS = ("ld", "lg", "st", "t", "v", "s", "g", "cc")


class Plan:
    def __init__(self):
        self.ops = {e: [] for e in ENGS}
        self.cnt = {s: 0 for s in SEMS}
        self.sem = {}
        self.bankw = {}
        self.hiwater = {}

    def op(self, eng, build, waits=(), inc=None, amt=None):
        after = None
        if inc is not None:
            if amt is None:
                amt = 16 if inc in ("ld", "lg", "st") else 1
            self.cnt[inc] += amt
            after = self.cnt[inc]
        hw = self.hiwater.setdefault(eng, {})
        w = []
        for (sname, v) in waits:
            if not v:
                continue
            if inc == sname:
                v2 = min(v, self.cnt[sname] - (amt or 1))
            else:
                v2 = v
            if v2 > hw.get(sname, -1):
                hw[sname] = v2
                w.append((sname, v2))
        w = tuple(w)

        def fn(h, _b=build, _w=w, _i=inc, _a=amt):
            for (sname, val) in _w:
                h.wait_ge(self.sem[sname], val)
            ins = _b(h)
            if _i is not None:
                assert ins is not None
                ins.then_inc(self.sem[_i], _a)
        self.ops[eng].append(fn)
        return after

    def bw(self, key):
        return list(self.bankw.get(key, []))

    def bset(self, key, *marks):
        self.bankw[key] = [m for m in marks if m is not None]


def build_program():
    nc = bass.Bass("TRN2", target_bir_lowering=False)
    P = Plan()

    xT = nc.dram_tensor("xT", [E, T], F32, kind="ExternalInput").ap()
    wq_h = nc.dram_tensor("wq_h", [128, 8192], F32, kind="ExternalInput").ap()
    wk_h = nc.dram_tensor("wk_h", [128, 8192], F32, kind="ExternalInput").ap()
    wv_h = nc.dram_tensor("wv_h", [128, 8192], F32, kind="ExternalInput").ap()
    wo_h = nc.dram_tensor("wo_h", [128, 8192], F32, kind="ExternalInput").ap()
    w8_h = nc.dram_tensor("w8_h", [128, 8192], F32, kind="ExternalInput").ap()
    proj_h = nc.dram_tensor("proj_h", [128, 256], F32, kind="ExternalInput").ap()
    bcol_h = nc.dram_tensor("bcol_h", [128, 40], F32, kind="ExternalInput").ap()
    bvb_h = nc.dram_tensor("bvb_h", [128, 1024], F32, kind="ExternalInput").ap()
    sc_h = nc.dram_tensor("sc_h", [1, 8], F32, kind="ExternalInput").ap()
    sel16_h = nc.dram_tensor("sel16_h", [128, 128], BF16, kind="ExternalInput").ap()
    selh_h = nc.dram_tensor("selh_h", [128, 256], BF16, kind="ExternalInput").ap()
    ones_f_h = nc.dram_tensor("ones_f_h", [128, 128], F32, kind="ExternalInput").ap()
    ones_b_h = nc.dram_tensor("ones_b_h", [128, 128], BF16, kind="ExternalInput").ap()
    idn_f_h = nc.dram_tensor("idn_f_h", [128, 128], F32, kind="ExternalInput").ap()
    ck_h = nc.dram_tensor("ck_h", [128, 4], F32, kind="ExternalInput").ap()
    maskt_h = nc.dram_tensor("maskt_h", [16, 1040], BF16, kind="ExternalInput").ap()

    xoT = nc.dram_tensor("xoT", [E, T], F32, kind="ExternalOutput").ap()
    x1_hbm = nc.dram_tensor("x1_hbm", [E, T], F32, kind="ExternalOutput").ap()
    q_hbm = nc.dram_tensor("q_hbm", [E, T], BF16, kind="ExternalOutput").ap()
    dbg = nc.dram_tensor("dbg", [128, 4112], F32, kind="ExternalOutput").ap()

    p_in_a = nc.dram_tensor("p_in_a", [65, 2056], F32)
    p_in_b = nc.dram_tensor("p_in_b", [65, 2056], F32)
    p_out_a = nc.dram_tensor("p_out_a", [65, 2056], F32)
    p_out_b = nc.dram_tensor("p_out_b", [65, 2056], F32)
    stab_in = nc.dram_tensor("stab_in", [1, 128], F32)
    stab_out = nc.dram_tensor("stab_out", [1, 128], F32)

    import contextlib
    stack = contextlib.ExitStack()
    sb = lambda name, shape, dt: stack.enter_context(nc.sbuf_tensor(name, shape, dt))
    ps = lambda name, shape: stack.enter_context(nc.psum_tensor(name, shape, F32))

    wq = sb("wq", [128, 8192], BF16)
    wk = sb("wk", [128, 8192], BF16)
    wv = sb("wv", [128, 8192], BF16)
    wo = sb("wo", [128, 8192], BF16)
    w8 = sb("w8", [128, 8192], F8)
    proj2b = sb("proj2b", [128, 256], BF16)
    sel16 = sb("sel16", [128, 128], BF16)
    selh = sb("selh", [128, 256], BF16)
    ones_f = sb("ones_f", [128, 128], F32)
    ones_b = sb("ones_b", [128, 128], BF16)
    idn_f = sb("idn_f", [128, 128], F32)
    ck = sb("ck", [128, 4], F32)
    bcol = sb("bcol", [128, 40], F32)
    hbf = sb("hbf", [128, 8], F32)
    bvb = sb("bvb", [128, 1024], BF16)
    scrow = sb("scrow", [1, 32], F32)
    smallcol = sb("smallcol", [128, 16], F32)
    rows = sb("rows", [1, 2048], F32)
    rows_b = sb("rows_b", [1, 512], BF16)
    bufA = sb("bufA", [128, 4096], F32)
    bufB = sb("bufB", [128, 4096], F32)
    bufD = sb("bufD", [128, 4096], BF16)
    kt = sb("kt", [128, 4096], BF16)
    q8t = sb("q8t", [128, 4096], F8)
    t1sc = sb("t1sc", [128, 1024], F32)
    sqch = sb("sqch", [128, 1024], BF16)
    v1t = sb("v1t", [128, 4160], BF16)
    ekt = sb("ekt", [128, 4112], BF16)
    cacc = sb("cacc", [128, 4112], F32)
    dgrows = sb("dgrows", [16, 512], F32)
    dcols = sb("dcols", [128, 32], F32)
    lnb = sb("lnb", [128, 32], F32)
    smax16 = sb("smax16", [128, 16], F32)
    rcb = sb("rcb", [128, 1024], BF16)
    rmcols = sb("rmcols", [128, 16], F32)
    rmT = sb("rmT", [16, 512], F32)
    t16 = sb("t16", [16, 512], F32)
    eps16 = sb("eps16", [16, 512], BF16)
    ctxden = sb("ctxden", [128, 2080], BF16)
    csxt = sb("csxt", [16, 1040], BF16)
    maskt = sb("maskt", [16, 1040], BF16)
    eqt = sb("eqt", [128, 2048], BF16)
    attnst = sb("attnst", [64, 4096], BF16)
    stg = sb("stg", [1, 64], F32)
    vcol = sb("vcol", [128, 4], F32)

    ps_a = ps("ps_a", [128, 512])
    ps_b = ps("ps_b", [128, 512])
    ps_c = ps("ps_c", [128, 512])
    ps_d = ps("ps_d", [128, 512])
    ps_e = ps("ps_e", [128, 512])
    ps_f = ps("ps_f", [128, 512])
    ps_g = ps("ps_g", [128, 512])
    ps_h = ps("ps_h", [128, 512])
    BK = {id(b): nm for b, nm in [(ps_a, "a"), (ps_b, "b"), (ps_c, "c"), (ps_d, "d"),
                                  (ps_e, "e"), (ps_f, "f"), (ps_g, "g"), (ps_h, "h")]}

    def wchunk(wt, oc, kc, width=128):
        base = (oc * 8 + kc) * 128
        return wt[:, base:base + width]

    def wv_moving(kc, half):
        v = wv[:].rearrange("p (o k c) -> p o k c", o=8, k=8)
        return v[:, half * 4:(half + 1) * 4, kc, :]

    P.op("sync", lambda s: s.dma_start(out=ones_f[:], in_=ones_f_h[:, :]), inc="ld")
    P.op("sync", lambda s: s.dma_start(out=ones_b[:], in_=ones_b_h[:, :]), inc="ld")
    P.op("sync", lambda s: s.dma_start(out=idn_f[:], in_=idn_f_h[:, :]), inc="ld")
    P.op("sync", lambda s: s.dma_start(out=ck[:], in_=ck_h[:, :]), inc="ld")
    P.op("sync", lambda s: s.dma_start(out=maskt[:], in_=maskt_h[:, :]), inc="ld")
    P.op("sync", lambda s: s.dma_start(out=sel16[:], in_=sel16_h[:, :]), inc="ld")
    P.op("sync", lambda s: s.dma_start(out=selh[:], in_=selh_h[:, :]), inc="ld")
    P.op("sync", lambda s: s.dma_start(out=bcol[:], in_=bcol_h[:, :]), inc="ld")
    P.op("sync", lambda s: s.dma_start(out=scrow[:, 0:8], in_=sc_h[:, :]), inc="ld")
    bvbm = P.op("gpsimd", lambda g: g.dma_start(out=bvb[:], in_=bvb_h[:, :]), inc="lg")
    wqm = P.op("gpsimd", lambda g: g.dma_start(out=wq[:], in_=wq_h[:, :]), inc="lg")
    wkm = P.op("gpsimd", lambda g: g.dma_start(out=wk[:], in_=wk_h[:, :]), inc="lg")
    wvm = P.op("gpsimd", lambda g: g.dma_start(out=wv[:], in_=wv_h[:, :]), inc="lg")
    wom = P.op("gpsimd", lambda g: g.dma_start(out=wo[:], in_=wo_h[:, :]), inc="lg")
    w8ld = []
    for half in range(2):
        w = P.op("gpsimd", lambda g, _h=half: g.dma_start(
            out=bufA[:], in_=w8_h[:, _h * 4096:(_h + 1) * 4096]),
            waits=[("v", w8ld[0])] if half == 1 else (), inc="lg")
        c = P.op("vector", lambda v, _h=half: v.tensor_copy(
            w8[:, _h * 4096:(_h + 1) * 4096], bufA[:]), waits=[("lg", w)], inc="v")
        w8ld.append(c)
    wp = P.op("sync", lambda s: s.dma_start(out=t1sc[:, 0:256], in_=proj_h[:, :]), inc="ld")
    proj_ready = P.op("scalar", lambda h: h.mul(proj2b[:], t1sc[:, 0:256], DSC),
                      waits=[("ld", wp)], inc="s")
    consts_ld = P.cnt["ld"]

    def _preload(h):
        from concourse.hw_specs import get_activation_tables
        tabs = get_activation_tables(nc.m.arch)
        sid = None
        for idx, funcs in enumerate(tabs.values()):
            if AF.Exp in funcs and AF.Ln in funcs:
                sid = idx
                break
        assert sid is not None
        return h.add_instruction(mybir.InstLoadActFuncSet(
            name=nc.get_next_instruction_name(), act_func_set_id=sid, ins=[], outs=[]))
    P.op("scalar", _preload)

    sc_ok = P.op("vector", lambda v: v.reciprocal(scrow[:, 16:24], scrow[:, 0:8]),
                 waits=[("ld", consts_ld)], inc="v")
    sc_ok = P.op("vector", lambda v: v.tensor_scalar(
        out=scrow[:, 8:16], in0=scrow[:, 0:8], scalar1=scrow[0:1, 1:2], scalar2=0.5,
        op0=ALU.mult, op1=ALU.mult), inc="v")
    t_b = P.op("tensor", lambda t: t.matmul(ps_a[:, 0:16], ones_f[0:1, 0:128],
                                            scrow[0:1, 0:16], start=True, stop=True),
               waits=[("v", sc_ok)], inc="t")
    sc_col = P.op("vector", lambda v: v.tensor_copy(smallcol[:, 0:16], ps_a[:, 0:16]),
                  waits=[("t", t_b)], inc="v")
    hbf_ok = P.op("vector", lambda v: v.tensor_scalar_mul(hbf[:], bcol[:, 32:40], 0.5),
                  waits=[("ld", consts_ld)], inc="v")
    P.op("vector", lambda v: v.memset(cacc[:], 0.0), inc="v")
    P.op("vector", lambda v: v.memset(smax16[:], -1e30), inc="v")
    P.op("vector", lambda v: v.memset(
        v1t[:].rearrange("p (s h c) -> p s h c", s=4, h=16)[:, :, :, 64:65], 1.0), inc="v")
    P.op("vector", lambda v: v.memset(
        ekt[:].rearrange("p (h c) -> p h c", h=16)[:, :, 256:257], 1.0), inc="v")
    setup_v = P.cnt["v"]
    P.bset("bank_a", ("v", sc_col))

    def ln_stats(buf, want_rinvs, extra_waits=(), src_bf16=False):
        ew = list(extra_waits)
        t_sx = None
        mmq = {}
        if src_bf16:
            def sxmm(t):
                last = None
                for e in range(8):
                    last = t.matmul(ps_a[0:1, :], ones_b[:, 0:1], buf[:, e * 512:(e + 1) * 512],
                                    start=(e == 0), stop=(e == 7))
                return last
            t_sx = P.op("tensor", sxmm,
                        waits=P.bw("bank_a") + [("v", P.cnt["v"]), ("s", P.cnt["s"])] + ew, inc="t")
        else:
            for e in range(8):
                hf = e % 4
                cm = P.op("vector", lambda v, _e=e, _hf=hf: v.tensor_copy(
                    eqt[:, _hf * 512:(_hf + 1) * 512], buf[:, _e * 512:(_e + 1) * 512]),
                    waits=([("t", mmq[e - 4])] if e >= 4 else P.bw("eqt")) + ew, inc="v")
                t_sx = P.op("tensor", lambda t, _e=e, _hf=hf: t.matmul(
                    ps_a[0:1, :], ones_b[:, 0:1], eqt[:, _hf * 512:(_hf + 1) * 512],
                    start=(_e == 0), stop=(_e == 7)),
                    waits=[("v", cm)] + (P.bw("bank_a") if e == 0 else []), inc="t")
                mmq[e] = t_sx
        t_sq = None
        for e in range(8):
            hf = e % 2
            s_sq = P.op("scalar", lambda h, _e=e, _hf=hf: h.activation(
                out=sqch[:, _hf * 512:(_hf + 1) * 512], in_=buf[:, _e * 512:(_e + 1) * 512],
                func=AF.Square, bias=0.0, scale=1.0),
                waits=([("t", mmq.get(100 + e - 2, P.cnt["t"]))] if e >= 2 else P.bw("sqch")) + ew,
                inc="s")
            t_sq = P.op("tensor", lambda t, _e=e, _hf=hf: t.matmul(
                ps_b[0:1, :], ones_b[:, 0:1], sqch[:, _hf * 512:(_hf + 1) * 512],
                start=(_e == 0), stop=(_e == 7)),
                waits=[("s", s_sq)] + (P.bw("bank_b") if e == 0 else []), inc="t")
            mmq[100 + e] = t_sq
        if not src_bf16:
            P.bset("eqt", ("t", P.cnt["t"]))
        P.bset("sqch", ("t", P.cnt["t"]))
        v1 = P.op("vector", lambda v: v.tensor_scalar_mul(rows[:, 0:512], ps_a[0:1, :], 1.0 / E),
                  waits=[("t", t_sx)], inc="v")
        v1b = P.op("vector", lambda v: v.tensor_scalar_mul(rows_b[:, 0:512], ps_a[0:1, :], 1.0 / E),
                   inc="v")
        P.bset("bank_a", ("v", v1b))
        v2 = P.op("vector", lambda v: v.tensor_scalar_mul(rows[:, 512:1024], ps_b[0:1, :], 1.0 / E),
                  waits=[("t", t_sq)], inc="v")
        P.bset("bank_b", ("v", v2))
        v3 = P.op("vector", lambda v: v.tensor_mul(rows[:, 1024:1536], rows[:, 0:512], rows[:, 0:512]),
                  inc="v")
        v4 = P.op("vector", lambda v: v.scalar_tensor_tensor(
            out=rows[:, 1024:1536], in0=rows[:, 512:1024], scalar=LNEPS,
            in1=rows[:, 1024:1536], op0=ALU.add, op1=ALU.subtract), inc="v")
        s1 = P.op("scalar", lambda h: h.activation(out=rows[:, 512:1024], in_=rows[:, 1024:1536],
                                                   func=AF.Ln, bias=0.0, scale=1.0),
                  waits=[("v", v4)], inc="s")
        s2 = P.op("scalar", lambda h: h.activation(out=rows[:, 1536:2048], in_=rows[:, 512:1024],
                                                   func=AF.Exp, bias=0.0, scale=-0.5),
                  inc="s")
        if want_rinvs:
            vr = P.op("vector", lambda v: v.tensor_scalar_mul(
                rows[:, 1024:1536], rows[:, 1536:2048], scrow[0:1, 16:17]),
                waits=[("s", s2)], inc="v")
            return vr, s2
        return None, s2

    def bcast_b(row_ap, bank, bankkey, n=512, wait_marks=()):
        return P.op("tensor", lambda t: t.matmul(bank[:, 0:n], ones_b[0:1, 0:128], row_ap,
                                                 start=True, stop=True),
                    waits=list(wait_marks) + P.bw(bankkey), inc="t")

    def bcast_f(row_ap, bank, bankkey, n=512, wait_marks=()):
        return P.op("tensor", lambda t: t.matmul(bank[:, 0:n], ones_f[0:1, 0:128], row_ap,
                                                 start=True, stop=True),
                    waits=list(wait_marks) + P.bw(bankkey), inc="t")

    st_x1 = {}
    st_q = {}
    prev_x1_store = None
    xload = {}

    def emit_lx(_it, first):
        lp = None
        for pc in range(4):
            lp = P.op("sync", lambda s, _pc=pc: s.dma_start(
                out=bufA[:, _pc * 1024:(_pc + 1) * 1024].rearrange("p (e t) -> p e t", e=2),
                in_=xT.rearrange("(e p) t -> p e t", p=128)[
                    :, 2 * _pc:2 * _pc + 2, _it * TS:(_it + 1) * TS]),
                waits=(([("v", setup_v)] if first else []) + P.bw("bufAx")) if pc == 0 else [],
                inc="ld")
        xload[_it] = lp
    emit_lx(0, True)
    for it in range(NT):
        lx = xload[it]
        vr, _ = ln_stats(bufA, want_rinvs=True, extra_waits=[("ld", lx)])
        bm = bcast_b(rows_b[0:1, 0:512], ps_b, "bank_b", wait_marks=[("v", P.cnt["v"])])
        bm_ = bm
        br = bcast_f(rows[0:1, 1024:1536], ps_a, "bank_a", wait_marks=[("v", vr)])
        qm = None
        qmark = {}
        for e in range(8):
            hf = e % 2
            P.op("vector", lambda v, _e=e, _hf=hf: v.scalar_tensor_tensor(
                out=t1sc[:, _hf * 512:(_hf + 1) * 512], in0=bufA[:, _e * 512:(_e + 1) * 512],
                scalar=1.0, in1=ps_b[:, :], op0=ALU.mult, op1=ALU.subtract),
                waits=[("t", bm)] + P.bw(f"t1sc{hf}"), inc="v")
            qm = P.op("vector", lambda v, _e=e, _hf=hf: v.tensor_mul(
                q8t[:, _e * 512:(_e + 1) * 512], t1sc[:, _hf * 512:(_hf + 1) * 512], ps_a[:, :]),
                waits=[("t", br)], inc="v")
            P.bset(f"t1sc{hf}", ("v", qm))
            qmark[e] = qm
        quant_done = qm
        P.bset("bank_a", ("v", quant_done))
        P.bset("bank_b", ("v", quant_done))
        for oc in range(8):
            bank = (ps_c, ps_d, ps_e, ps_f)[oc % 4]
            bkey = "bank_" + BK[id(bank)]
            tm = None
            for kp in range(4):
                tm = P.op("tensor", lambda t, _oc=oc, _kp=kp, _bank=bank: t.matmul(
                    _bank[:, :],
                    wchunk(w8, _oc, 2 * _kp, width=256).rearrange("p (k c) -> p k c", k=2),
                    q8t[:, _kp * 1024:(_kp + 1) * 1024].rearrange("p (k n) -> p k n", k=2),
                    start=(_kp == 0), stop=(_kp == 3),
                    perf_mode=mybir.MatmulPerfMode.DoubleRow),
                    waits=([("v", qmark[2 * kp + 1])] if oc == 0 else []) +
                          (P.bw(bkey) if kp == 0 else []),
                    inc="t" if kp == 3 else None)
            sm = P.op("scalar", lambda h, _oc=oc, _bank=bank: h.activation(
                out=t1sc[:, (_oc % 2) * 512:(_oc % 2 + 1) * 512], in_=_bank[:, :], func=AF.Identity,
                bias=hbf[:, _oc:_oc + 1], scale=smallcol[:, 8:9]),
                waits=[("t", tm)] + P.bw(f"t1sc{oc % 2}"), inc="s")
            P.bset(bkey, ("s", sm))
            va = P.op("vector", lambda v, _oc=oc: v.tensor_add(
                bufB[:, _oc * 512:(_oc + 1) * 512], bufA[:, _oc * 512:(_oc + 1) * 512],
                t1sc[:, (_oc % 2) * 512:(_oc % 2 + 1) * 512]),
                waits=[("s", sm)] + ([("st", prev_x1_store)] if (oc == 0 and prev_x1_store) else []),
                inc="v")
            P.bset(f"t1sc{oc % 2}", ("v", va))
        x1_done = P.cnt["v"]
        P.bset("bufAx", ("v", x1_done), ("t", P.cnt["t"]), ("s", P.cnt["s"]))
        st1 = None
        for pc in range(4):
            st1 = P.op("sync", lambda s, _it=it, _pc=pc: s.dma_start(
                out=x1_hbm.rearrange("(e p) t -> p e t", p=128)[
                    :, 2 * _pc:2 * _pc + 2, _it * TS:(_it + 1) * TS],
                in_=bufB[:, _pc * 1024:(_pc + 1) * 1024].rearrange("p (e t) -> p e t", e=2)),
                waits=[("v", x1_done)] if pc == 0 else [], inc="st")
        st_x1[it] = st1
        prev_x1_store = st1
        if it + 1 < NT:
            emit_lx(it + 1, False)
        vr2, s_r2 = ln_stats(bufB, want_rinvs=False)
        bm2 = bcast_b(rows_b[0:1, 0:512], ps_b, "bank_b", wait_marks=[("v", P.cnt["v"])])
        br2 = bcast_f(rows[0:1, 1536:2048], ps_a, "bank_a", wait_marks=[("s", s_r2)])
        ym = None
        ymark = {}
        for e in range(8):
            hf = e % 2
            P.op("vector", lambda v, _e=e, _hf=hf: v.scalar_tensor_tensor(
                out=t1sc[:, _hf * 512:(_hf + 1) * 512], in0=bufB[:, _e * 512:(_e + 1) * 512],
                scalar=1.0, in1=ps_b[:, :], op0=ALU.mult, op1=ALU.subtract),
                waits=[("t", bm2)] + P.bw(f"t1sc{hf}"), inc="v")
            ym = P.op("vector", lambda v, _e=e, _hf=hf: v.tensor_mul(
                bufD[:, _e * 512:(_e + 1) * 512], t1sc[:, _hf * 512:(_hf + 1) * 512], ps_a[:, :]),
                waits=[("t", br2)] + (P.bw("bufD") if e == 0 else []), inc="v")
            P.bset(f"t1sc{hf}", ("v", ym))
            ymark[e] = ym
        y2_done = ym
        P.bset("bank_a", ("v", y2_done))
        P.bset("bank_b", ("v", y2_done))
        for oc in range(8):
            bank = (ps_c, ps_d, ps_e, ps_f)[oc % 4]
            bkey = "bank_" + BK[id(bank)]

            tq = None
            for kc in range(8):
                tq = P.op("tensor", lambda t, _oc=oc, _kc=kc, _bank=bank: t.matmul(
                    _bank[:, :], wchunk(wq, _oc, _kc), bufD[:, _kc * 512:(_kc + 1) * 512],
                    start=(_kc == 0), stop=(_kc == 7)),
                    waits=([("v", ymark[kc]), ("lg", wqm)] if oc == 0 else []) +
                          (P.bw(bkey) if kc == 0 else []),
                    inc="t" if kc == 7 else None)
            sq_ = P.op("scalar", lambda h, _oc=oc, _bank=bank: h.activation(
                out=eqt[:, (_oc % 2) * 512:(_oc % 2 + 1) * 512], in_=_bank[:, :], func=AF.Identity,
                bias=bcol[:, _oc:_oc + 1], scale=1.0),
                waits=[("t", tq)] + P.bw(f"qch{oc % 2}") + (P.bw("eqt") if oc < 2 else []), inc="s")
            stq = P.op("sync", lambda s, _it=it, _oc=oc: s.dma_start(
                out=q_hbm.rearrange("(o p) t -> p o t", p=128)[:, _oc, _it * TS:(_it + 1) * TS],
                in_=eqt[:, (_oc % 2) * 512:(_oc % 2 + 1) * 512]),
                waits=[("s", sq_)], inc="st")
            st_q[(it, oc)] = stq
            P.bset(f"qch{oc % 2}", ("st", stq))

            def k_mm(t, _oc=oc, _bank=bank):
                last = None
                for kc in range(8):
                    last = t.matmul(_bank[:, :], wchunk(wk, _oc, kc), bufD[:, kc * 512:(kc + 1) * 512],
                                    start=(kc == 0), stop=(kc == 7))
                return last
            tk = P.op("tensor", k_mm,
                      waits=[("s", sq_)] + ([("lg", wkm)] if oc == 0 else []) +
                            (P.bw("kt") if oc == 0 else []), inc="t")
            ssq = P.op("scalar", lambda h, _oc=oc, _bank=bank: h.activation(
                out=sqch[:, (_oc % 2) * 512:(_oc % 2 + 1) * 512], in_=_bank[:, :],
                func=AF.Square, bias=bcol[:, 8 + _oc:9 + _oc], scale=1.0),
                waits=[("t", tk)] + (P.bw("sqch") if oc < 2 else [("t", P.cnt["t"])]), inc="s")
            sk_ = P.op("scalar", lambda h, _oc=oc, _bank=bank: h.activation(
                out=kt[:, _oc * 512:(_oc + 1) * 512], in_=_bank[:, :], func=AF.Identity,
                bias=bcol[:, 8 + _oc:9 + _oc], scale=1.0), inc="s")
            P.bset(bkey, ("s", sk_))
            P.op("tensor", lambda t, _oc=oc: t.matmul(
                ps_h[0:16, :], sel16[:, _oc * 16:(_oc + 1) * 16],
                sqch[:, (_oc % 2) * 512:(_oc % 2 + 1) * 512],
                start=(_oc == 0), stop=(_oc == 7)),
                waits=[("s", ssq)] + (P.bw("bank_h") if oc == 0 else []), inc="t")
        P.bset("sqch", ("t", P.cnt["t"]))
        dg_v = P.op("vector", lambda v: v.tensor_copy(dgrows[:, :], ps_h[0:16, :]),
                    waits=[("t", P.cnt["t"])], inc="v")
        P.bset("bank_h", ("v", dg_v))
        vdone = {}
        for sub in range(NSUB):
            for half in range(2):
                bank = (ps_c, ps_d, ps_e, ps_f)[(sub * 2 + half) % 4]
                bkey = "bank_" + BK[id(bank)]

                def v_mm(t, _sub=sub, _hf=half, _bank=bank):
                    last = None
                    for kc in range(8):
                        last = t.matmul(_bank[:, :],
                                        bufD[:, kc * 512 + _sub * 128: kc * 512 + (_sub + 1) * 128],
                                        wv_moving(kc, _hf), start=(kc == 0), stop=(kc == 7))
                    return last
                tv = P.op("tensor", v_mm,
                          waits=([("lg", wvm)] if (sub == 0 and half == 0) else []) + P.bw(bkey) +
                                (P.bw("v1t") if (sub == 0 and half == 0) else []), inc="t")
                ve = P.op("vector", lambda v, _sub=sub, _hf=half, _bank=bank: v.scalar_tensor_tensor(
                    out=v1t[:].rearrange("p (s h c) -> p s h c", s=4, h=16)[
                        :, _sub, _hf * 8:(_hf + 1) * 8, 0:64],
                    in0=_bank[:].rearrange("p (h c) -> p h c", h=8),
                    scalar=1.0, in1=bvb[:, _hf * 512:(_hf + 1) * 512].rearrange(
                        "p (h c) -> p h c", h=8),
                    op0=ALU.mult, op1=ALU.add),
                    waits=[("t", tv)] + ([("lg", bvbm)] if (sub == 0 and half == 0) else []), inc="v")
                P.bset(bkey, ("v", ve))
                vdone[(sub, half)] = ve
        ddbanks = [(ps_e, ps_f), (ps_c, ps_d)]

        def emit_dctr(_sub):
            _par = _sub % 2
            _dc = _par * 16
            tt = P.op("tensor", lambda t: t.transpose(
                ps_b[:, 0:16], dgrows[:, _sub * 128:(_sub + 1) * 128], idn_f[0:16, 0:16]),
                waits=[("v", dg_v)] + P.bw("bank_b"), inc="t")
            dcv = P.op("vector", lambda v: v.tensor_copy(
                dcols[:, _dc:_dc + 16], ps_b[:, 0:16]),
                waits=[("t", tt)] + P.bw(f"dcols{_par}"), inc="v")
            P.bset("bank_b", ("v", dcv))
            dctr[_sub] = dcv
        dctr = {}
        emit_dctr(0)
        for sub in range(NSUB):
            par = sub % 2
            eko = 0
            dco = par * 16
            dc_v = dctr[sub]
            expm = {}
            last_c = None

            def emit_c(h_, _sub=sub, _eko=eko):
                nonlocal last_c
                bank = (ps_g, ps_h, ps_a)[h_ % 3]
                bkey = "bank_" + BK[id(bank)]
                tc_ = P.op("tensor", lambda t, _h=h_, _s=_sub, _e=_eko, _bank=bank: t.matmul(
                    _bank[0:65, 0:257], v1t[:, _s * 1040 + _h * 65:_s * 1040 + (_h + 1) * 65],
                    ekt[:, _e + _h * 257:_e + (_h + 1) * 257], start=True, stop=True),
                    waits=[("s", expm[h_]), ("v", vdone[(_sub, 1)])] + P.bw(bkey), inc="t")
                fl = P.op("vector", lambda v, _h=h_, _bank=bank: v.tensor_add(
                    cacc[0:65, _h * 257:(_h + 1) * 257], cacc[0:65, _h * 257:(_h + 1) * 257],
                    _bank[0:65, 0:257]), waits=[("t", tc_)], inc="v")
                P.bset(bkey, ("v", fl))
                last_c = tc_

            for hp in range(8):
                bA, bB = ddbanks[hp % 2]
                kA, kB = "bank_" + BK[id(bA)], "bank_" + BK[id(bB)]
                hA, hB = 2 * hp, 2 * hp + 1
                base = hp * 512 + sub * 128
                tA = P.op("tensor", lambda t, _b=base, _bA=bA: t.matmul(
                    _bA[:, 0:256], kt[0:64, _b:_b + 128], proj2b[0:64, :],
                    start=True, stop=True),
                    waits=P.bw(kA) + ([("v", dc_v)] if hp == 0 else []), inc="t")
                tB = P.op("tensor", lambda t, _b=base, _bB=bB: t.matmul(
                    _bB[:, 0:256], kt[64:128, _b:_b + 128], proj2b[64:128, :],
                    start=True, stop=True, tile_position=(64, 0)),
                    waits=P.bw(kB), inc="t")
                ekw = P.bw("ekt") if hp == 0 else []
                eA = P.op("scalar", lambda h, _hA=hA, _bA=bA, _e=eko, _dc=dco: h.activation(
                    out=ekt[:, _e + _hA * 257:_e + _hA * 257 + 256], in_=_bA[:, 0:256], func=AF.Exp,
                    bias=dcols[:, _dc + _hA:_dc + _hA + 1], scale=1.0),
                    waits=[("t", tA)] + ekw, inc="s")
                P.bset(kA, ("s", eA))
                eB = P.op("scalar", lambda h, _hB=hB, _bB=bB, _e=eko, _dc=dco: h.activation(
                    out=ekt[:, _e + _hB * 257:_e + _hB * 257 + 256], in_=_bB[:, 0:256], func=AF.Exp,
                    bias=dcols[:, _dc + _hB:_dc + _hB + 1], scale=1.0),
                    waits=[("t", tB)], inc="s")
                P.bset(kB, ("s", eB))
                expm[hA] = eA
                expm[hB] = eB
                if hp == 4:
                    vr1 = P.op("vector", lambda v, _dc=dco: v.reduce_max(
                        lnb[:, _dc:_dc + 8].rearrange("p (h o) -> p h o", h=8),
                        ekt[:, 0:2056].rearrange("p (h c) -> p h c", h=8)[:, :, 0:256],
                        axis=mybir.AxisListType.X), waits=[("s", expm[7])], inc="v")
                if hp >= 1:
                    emit_c(2 * (hp - 1))
                    emit_c(2 * (hp - 1) + 1)
            ek_done = P.cnt["s"]
            if sub + 1 < NSUB:
                emit_dctr(sub + 1)
            vr2_ = P.op("vector", lambda v, _dc=dco: v.reduce_max(
                lnb[:, _dc + 8:_dc + 16].rearrange("p (h o) -> p h o", h=8),
                ekt[:, 2056:4112].rearrange("p (h c) -> p h c", h=8)[:, :, 0:256],
                axis=mybir.AxisListType.X), waits=[("s", ek_done)], inc="v")
            v_rm = vr2_
            s_ln = P.op("scalar", lambda h, _dc=dco: h.activation(
                out=lnb[:, _dc:_dc + 16], in_=lnb[:, _dc:_dc + 16],
                func=AF.Ln, bias=0.0, scale=1.0),
                waits=[("v", v_rm)], inc="s")
            v_sb = P.op("vector", lambda v, _dc=dco: v.tensor_sub(
                lnb[:, _dc:_dc + 16], lnb[:, _dc:_dc + 16], dcols[:, _dc:_dc + 16]),
                waits=[("s", s_ln)], inc="v")
            vsm = P.op("vector", lambda v, _dc=dco: v.tensor_max(
                smax16[:, :], smax16[:, :], lnb[:, _dc:_dc + 16]), inc="v")
            P.bset(f"dcols{par}", ("v", vsm))
            emit_c(14)
            emit_c(15)
            P.bset("ekt", ("t", last_c), ("v", v_rm))
        P.bset("v1t", ("t", P.cnt["t"]))
        P.bset("kt", ("t", P.cnt["t"]))
        P.bset("bufD", ("t", P.cnt["t"]))
    v_f1 = P.op("vector", lambda v: v.reduce_max(vcol[:, 0:1], smax16[:, :],
                                                 axis=mybir.AxisListType.X), inc="v")
    t_f1 = P.op("tensor", lambda t: t.transpose(ps_g[0:1, 0:128], vcol[:, 0:1], idn_f[:, :]),
                waits=[("v", v_f1)] + P.bw("bank_g"), inc="t")
    v_f2 = P.op("vector", lambda v: v.memset(stg[:, :], 0.0), inc="v")
    v_f3 = P.op("vector", lambda v: v.reduce_max(stg[0:1, 0:1], ps_g[0:1, 0:128],
                                                 axis=mybir.AxisListType.X),
                waits=[("t", t_f1), ("v", v_f2)], inc="v")
    P.bset("bank_g", ("v", v_f3))

    prev_out_store = None
    cload = {}

    def emit_lq(_it):
        lq = None
        for pc in range(4):
            lq = P.op("gpsimd", lambda g, _pc=pc: g.dma_start(
                out=kt[:, _pc * 1024:(_pc + 1) * 1024].rearrange("p (o t) -> p o t", o=2),
                in_=q_hbm.rearrange("(o p) t -> p o t", p=128)[
                    :, 2 * _pc:2 * _pc + 2, _it * TS:(_it + 1) * TS]),
                waits=(([("st", st_q[(NT - 1, 7)])] if _it == 0 else []) + P.bw("kt"))
                      if pc == 0 else [],
                inc="lg")
        cload[_it] = lq

    def emit_lx1(_it, extra):
        lx1 = None
        for pc in range(4):
            lx1 = P.op("gpsimd", lambda g, _pc=pc: g.dma_start(
                out=bufB[:, _pc * 1024:(_pc + 1) * 1024].rearrange("p (e t) -> p e t", e=2),
                in_=x1_hbm.rearrange("(e p) t -> p e t", p=128)[
                    :, 2 * _pc:2 * _pc + 2, _it * TS:(_it + 1) * TS]),
                waits=(list(extra) + [("st", st_x1[_it])]) if pc == 0 else [], inc="lg")
        cload[(_it, "x1")] = lx1

    d2 = P.op("gpsimd", lambda g: g.dma_start(out=stab_in[0:1, 0:64], in_=stg[:, :]),
              waits=[("v", v_f3)], inc="lg")
    d1a = P.op("gpsimd", lambda g: g.dma_start(out=p_in_a[:, :], in_=cacc[0:65, 0:2056]),
               waits=[("v", P.cnt["v"])], inc="lg")
    d1b = P.op("gpsimd", lambda g: g.dma_start(out=p_in_b[:, :], in_=cacc[0:65, 2056:4112]),
               inc="lg")
    emit_lq(0)
    emit_lx1(0, [])
    cc2 = P.op("gpsimd", lambda g: g.collective_compute(
        "AllReduce", ALU.max, replica_groups=[[0, 1, 2, 3, 4, 5, 6, 7]],
        ins=[stab_in.ap().opt()], outs=[stab_out.ap().opt()]), waits=[("lg", d2)], inc="cc")
    cc1a = P.op("gpsimd", lambda g: g.collective_compute(
        "AllReduce", ALU.add, replica_groups=[[0, 1, 2, 3], [4, 5, 6, 7]],
        ins=[p_in_a.ap().opt()], outs=[p_out_a.ap().opt()]), waits=[("lg", d1a)], inc="cc")
    cc1b = P.op("gpsimd", lambda g: g.collective_compute(
        "AllReduce", ALU.add, replica_groups=[[0, 1, 2, 3], [4, 5, 6, 7]],
        ins=[p_in_b.ap().opt()], outs=[p_out_b.ap().opt()]), waits=[("lg", d1b)], inc="cc")
    d4 = P.op("gpsimd", lambda g: g.dma_start(out=stg[:, :], in_=stab_out[0:1, 0:64]),
              waits=[("cc", cc2)], inc="lg")
    d3a = P.op("gpsimd", lambda g: g.dma_start(out=cacc[0:65, 0:2056], in_=p_out_a[:, :]),
               waits=[("cc", cc1a)], inc="lg")
    d3b = P.op("gpsimd", lambda g: g.dma_start(out=cacc[0:65, 2056:4112], in_=p_out_b[:, :]),
               waits=[("cc", cc1b)], inc="lg")
    _asm = {}

    def emit_assembly():
        s_s1 = P.op("scalar", lambda h: h.activation(out=scrow[:, 24:32], in_=stg[0:1, 0:8],
                                                     func=AF.Exp, bias=0.0, scale=-1.0),
                    waits=[("lg", d4)], inc="s")
        t_s1 = P.op("tensor", lambda t: t.matmul(ps_a[:, 0:4], ones_f[0:1, 0:128], scrow[0:1, 24:28],
                                                 start=True, stop=True),
                    waits=[("s", s_s1)] + P.bw("bank_a"), inc="t")
        v_s1 = P.op("vector", lambda v: v.tensor_copy(smallcol[:, 12:16], ps_a[:, 0:4]),
                    waits=[("t", t_s1)], inc="v")
        P.bset("bank_a", ("v", v_s1))
        P.op("sync", lambda s: s.dma_start(out=dbg[0:65, :], in_=cacc[0:65, :]),
             waits=[("lg", d3b)], inc="st")
        csx_t = None
        for h_ in range(16):
            dwait = d3a if h_ < 8 else d3b
            tv1 = P.op("tensor", lambda t, _h=h_: t.transpose(
                ps_b[0:1, 0:65], cacc[0:65, _h * 257 + 256:_h * 257 + 257], idn_f[0:65, 0:65]),
                waits=[("lg", dwait)] + P.bw("bank_b"), inc="t")
            vv1 = P.op("vector", lambda v: v.tensor_scalar_mul(rows[:, 0:65], ps_b[0:1, 0:65], EPS),
                       waits=[("t", tv1)], inc="v")
            vv1b = P.op("vector", lambda v: v.memset(rows[:, 64:65], EPS * NTOK), inc="v")
            P.bset("bank_b", ("v", vv1b))
            tb = P.op("tensor", lambda t: t.matmul(ps_b[:, 0:65], ones_f[0:1, 0:128], rows[0:1, 0:65],
                                                   start=True, stop=True),
                      waits=[("v", vv1b)] + P.bw("bank_b"), inc="t")
            vb = P.op("vector", lambda v: v.tensor_copy(rcb[:, 0:65], ps_b[:, 0:65]),
                      waits=[("t", tb)], inc="v")
            P.bset("bank_b", ("v", vb))
            cm = []
            for c in range(2):
                ta = P.op("tensor", lambda t, _h=h_, _c=c: t.transpose(
                    ps_a[:, 0:65], cacc[0:65, _h * 257 + _c * 128:_h * 257 + (_c + 1) * 128],
                    idn_f[0:65, 0:65]), waits=P.bw("bank_a"), inc="t")
                vc_ = P.op("vector", lambda v, _h=h_, _c=c: v.scalar_tensor_tensor(
                    out=ctxden[:, (_h * 2 + _c) * 65:(_h * 2 + _c + 1) * 65], in0=ps_a[:, 0:65],
                    scalar=smallcol[:, 12:13], in1=rcb[:, 0:65], op0=ALU.mult, op1=ALU.add),
                    waits=[("t", ta), ("v", vb), ("v", v_s1)], inc="v")
                P.bset("bank_a", ("v", vc_))
                cm.append(vc_)
            for c in range(2):
                csx_t = P.op("tensor", lambda t, _h=h_, _c=c: t.matmul(
                    ps_h[0:16, 0:65], selh[:, _h * 16:(_h + 1) * 16],
                    ctxden[:, (_h * 2 + _c) * 65:(_h * 2 + _c + 1) * 65],
                    start=(_h == 0 and _c == 0), stop=(_h == 15 and _c == 1)),
                    waits=[("v", cm[c])] + (P.bw("bank_h") if (h_ == 0 and c == 0) else []),
                    inc="t")
        vcs = P.op("vector", lambda v: v.tensor_mul(
            csxt[:].rearrange("p (h c) -> p h c", h=16), maskt[:].rearrange("p (h c) -> p h c", h=16),
            ps_h[0:16, 0:65].unsqueeze(1).broadcast_to([16, 16, 65])),
            waits=[("t", csx_t)], inc="v")
        P.bset("bank_h", ("v", vcs))
        _asm["vcs"] = vcs


    for it in range(NT):
        lq = cload[it]
        lx1 = cload[(it, "x1")]
        for oc in range(8):
            hf = oc % 2
            if oc % 2 == 0:
                ssq = P.op("vector", lambda v, _oc=oc, _hf=hf: v.tensor_mul(
                    sqch[:, _hf * 512:(_hf + 1) * 512], kt[:, _oc * 512:(_oc + 1) * 512],
                    kt[:, _oc * 512:(_oc + 1) * 512]),
                    waits=[("lg", lq)] + (P.bw("sqch") if oc < 2 else [("t", P.cnt["t"])]), inc="v")
                sqw = ("v", ssq)
            else:
                ssq = P.op("scalar", lambda h, _oc=oc, _hf=hf: h.activation(
                    out=sqch[:, _hf * 512:(_hf + 1) * 512], in_=kt[:, _oc * 512:(_oc + 1) * 512],
                    func=AF.Square, bias=0.0, scale=1.0),
                    waits=[("lg", lq)] + (P.bw("sqch") if oc < 2 else [("t", P.cnt["t"])]), inc="s")
                sqw = ("s", ssq)
            P.op("tensor", lambda t, _oc=oc, _hf=hf: t.matmul(
                ps_h[0:16, :], sel16[:, _oc * 16:(_oc + 1) * 16],
                sqch[:, _hf * 512:(_hf + 1) * 512],
                start=(_oc == 0), stop=(_oc == 7)),
                waits=[sqw] + (P.bw("bank_h") if oc == 0 else []), inc="t")
        P.bset("sqch", ("t", P.cnt["t"]))
        dg_v = P.op("vector", lambda v: v.tensor_copy(dgrows[:, :], ps_h[0:16, :]),
                    waits=[("t", P.cnt["t"])] + P.bw("dgrows"), inc="v")
        P.bset("bank_h", ("v", dg_v))
        rmbanks = [(ps_e, ps_f), (ps_c, ps_d)]
        for sub in range(NSUB):
            for hp in range(8):
                bA, bB = rmbanks[hp % 2]
                kA, kB = "bank_" + BK[id(bA)], "bank_" + BK[id(bB)]
                hA, hB = 2 * hp, 2 * hp + 1
                base = hp * 512 + sub * 128
                tA = P.op("tensor", lambda t, _b=base, _bA=bA: t.matmul(
                    _bA[:, 0:256], kt[0:64, _b:_b + 128], proj2b[0:64, :],
                    start=True, stop=True), waits=P.bw(kA), inc="t")
                tB = P.op("tensor", lambda t, _b=base, _bB=bB: t.matmul(
                    _bB[:, 0:256], kt[64:128, _b:_b + 128], proj2b[64:128, :],
                    start=True, stop=True, tile_position=(64, 0)), waits=P.bw(kB), inc="t")
                rA = P.op("vector", lambda v, _hA=hA, _bA=bA: v.reduce_max(
                    rmcols[:, _hA:_hA + 1], _bA[:, 0:256], axis=mybir.AxisListType.X),
                    waits=[("t", tA)] + (P.bw("rmcols") if (sub == 0 and hp == 0) else []), inc="v")
                P.bset(kA, ("v", rA))
                rB = P.op("vector", lambda v, _hB=hB, _bB=bB: v.reduce_max(
                    rmcols[:, _hB:_hB + 1], _bB[:, 0:256], axis=mybir.AxisListType.X),
                    waits=[("t", tB)], inc="v")
                P.bset(kB, ("v", rB))
            rm_v = P.cnt["v"]
            trm = P.op("tensor", lambda t: t.transpose(ps_g[0:16, 0:128], rmcols[:, :], idn_f[:, :]),
                       waits=[("v", rm_v)] + P.bw("bank_g"), inc="t")
            vtm = P.op("vector", lambda v, _sub=sub: v.tensor_copy(
                rmT[:, _sub * 128:(_sub + 1) * 128], ps_g[0:16, 0:128]),
                waits=[("t", trm)], inc="v")
            P.bset("bank_g", ("v", vtm))
        ve1 = P.op("vector", lambda v: v.scalar_tensor_tensor(
            out=t16[:, :], in0=rmT[:, :], scalar=-K0, in1=dgrows[:, :],
            op0=ALU.add, op1=ALU.subtract),
            waits=[("v", P.cnt["v"])], inc="v")
        P.bset("dgrows", ("v", ve1))
        P.bset("rmcols", ("v", ve1))
        se1 = P.op("scalar", lambda h: h.activation(out=eps16[:, :], in_=t16[:, :],
                                                    func=AF.Exp, bias=0.0, scale=1.0),
                   waits=[("v", ve1)] + P.bw("eps16"), inc="s")
        if it == 0:
            emit_assembly()
        vcs = _asm["vcs"]
        pending_ln = []
        pending_div = []
        for hp in range(8):
            while pending_div:
                pending_div.pop(0)()
            hA, hB = 2 * hp, 2 * hp + 1
            qbase = hp * 512
            banksA = (ps_c, ps_d)
            banksB = (ps_e, ps_f)
            em = []
            for c in range(2):
                kA = "bank_" + BK[id(banksA[c])]
                kB = "bank_" + BK[id(banksB[c])]
                tfa = P.op("tensor", lambda t, _c=c, _qb=qbase, _bk=banksA: t.matmul(
                    _bk[_c][:, :], proj2b[0:64, _c * 128:(_c + 1) * 128], kt[0:64, _qb:_qb + 512],
                    start=True, stop=True), waits=P.bw(kA), inc="t")
                ea = P.op("scalar", lambda h, _c=c, _bk=banksA: h.activation(
                    out=eqt[:, _c * 512:(_c + 1) * 512], in_=_bk[_c][:, :], func=AF.Exp,
                    bias=ck[:, 0:1], scale=1.0),
                    waits=[("t", tfa)] + (P.bw("eqt") if (hp == 0 and c == 0) else []), inc="s")
                P.bset(kA, ("s", ea))
                tfb = P.op("tensor", lambda t, _c=c, _qb=qbase, _bk=banksB: t.matmul(
                    _bk[_c][:, :], proj2b[64:128, _c * 128:(_c + 1) * 128], kt[64:128, _qb:_qb + 512],
                    start=True, stop=True, tile_position=(64, 0)), waits=P.bw(kB), inc="t")
                eb = P.op("scalar", lambda h, _c=c, _bk=banksB: h.activation(
                    out=eqt[:, 1024 + _c * 512:1024 + (_c + 1) * 512], in_=_bk[_c][:, :], func=AF.Exp,
                    bias=ck[:, 0:1], scale=1.0), waits=[("t", tfb)], inc="s")
                P.bset(kB, ("s", eb))
                em.append((ea, eb))
            if pending_ln:
                pending_div.append(pending_ln.pop(0)())

            def numden(_h, _eqoff, _bank, _bkey, _em, extra):
                P.op("tensor", lambda t: t.matmul(
                    _bank[0:65, :], ctxden[:, (_h * 2) * 65:(_h * 2 + 1) * 65],
                    eqt[:, _eqoff:_eqoff + 512], start=True, stop=False),
                    waits=[("s", _em[0]), ("s", se1)] + P.bw(_bkey) + extra)
                P.op("tensor", lambda t: t.matmul(
                    _bank[0:65, :], ctxden[:, (_h * 2 + 1) * 65:(_h * 2 + 2) * 65],
                    eqt[:, _eqoff + 512:_eqoff + 1024], start=False, stop=False),
                    waits=[("s", _em[1])])
                return P.op("tensor", lambda t: t.matmul(
                    _bank[0:65, :], csxt[:, _h * 65:(_h + 1) * 65],
                    eps16[:, :], start=False, stop=True), inc="t")
            tnA = numden(hA, 0, ps_g, "bank_g", [em[0][0], em[1][0]],
                         [("v", vcs)] if (it == 0 and hp == 0) else [])
            tnB = numden(hB, 1024, ps_h, "bank_h", [em[0][1], em[1][1]], [])
            slA = (hp % 2) * 1024
            slB = (hp % 2) * 1024 + 512
            cpA = P.op("vector", lambda v, _sl=slA: v.tensor_copy(
                bufA[0:65, _sl:_sl + 512], ps_g[0:65, :]),
                waits=[("t", tnA)] + P.bw(f"nsl{hp % 2}A"), inc="v")
            P.bset("bank_g", ("v", cpA))
            cpB = P.op("vector", lambda v, _sl=slB: v.tensor_copy(
                bufA[0:65, _sl:_sl + 512], ps_h[0:65, :]),
                waits=[("t", tnB)] + P.bw(f"nsl{hp % 2}B"), inc="v")
            P.bset("bank_h", ("v", cpB))
            def _emit_ln(_hp=hp, _cpB=cpB, _slA=slA, _slB=slB, _it=it):
                sl0 = min(_slA, _slB)
                par = _hp % 2
                rowbuf = sqch if par == 0 else rcb
                P.op("scalar", lambda h, _s0=sl0: h.activation(
                    out=t1sc[64:65, 0:1024], in_=bufA[64:65, _s0:_s0 + 1024], func=AF.Ln,
                    bias=0.0, scale=1.0),
                    waits=[("v", _cpB)] +
                          ((P.bw("t1sc0") + P.bw("t1sc1")) if _hp == 0 else []), inc="s")
                scAB = P.op("scalar", lambda h, _rb=rowbuf: h.activation(
                    out=_rb[64:65, 0:1024], in_=t1sc[64:65, 0:1024], func=AF.Exp, bias=0.0,
                    scale=-1.0),
                    waits=P.bw(f"drow{par}") +
                          (P.bw("sqch") if (par == 0 and _hp < 2) else []) +
                          ([("v", _asm["vcs"])] if (_it == 0 and _hp == 1) else []),
                    inc="s")

                def _emit_div(_rb=rowbuf, _par=par):
                    bA_ = P.op("tensor", lambda t: t.matmul(ps_a[0:64, :], ones_b[64:65, 0:64],
                                                            _rb[64:65, 0:512], start=True, stop=True,
                                                            tile_position=(64, 0)),
                               waits=[("s", scAB)] + P.bw("bank_a"), inc="t")
                    vrA = P.op("vector", lambda v: v.tensor_copy(rcb[0:64, 0:512], ps_a[0:64, :]),
                               waits=[("t", bA_)] + P.bw("rcbA"), inc="v")
                    P.bset("bank_a", ("v", vrA))
                    evA = P.op("vector", lambda v: v.tensor_mul(
                        bufD[0:64, _hp * 512:(_hp + 1) * 512], bufA[0:64, _slA:_slA + 512],
                        rcb[0:64, 0:512]),
                        waits=[("v", vrA)] + (P.bw("bufDc") if _hp == 0 else []), inc="v")
                    P.bset(f"nsl{_hp % 2}A", ("v", evA))
                    P.bset("rcbA", ("v", evA))
                    bB_ = P.op("tensor", lambda t: t.matmul(ps_b[0:64, :], ones_b[64:65, 0:64],
                                                            _rb[64:65, 512:1024], start=True,
                                                            stop=True, tile_position=(64, 0)),
                               waits=[("s", scAB)] + P.bw("bank_b"), inc="t")
                    P.bset(f"drow{_par}", ("t", bB_))
                    vrB = P.op("vector", lambda v: v.tensor_copy(rcb[0:64, 512:1024], ps_b[0:64, :]),
                               waits=[("t", bB_)] + P.bw("rcbB"), inc="v")
                    P.bset("bank_b", ("v", vrB))
                    evB = P.op("vector", lambda v: v.tensor_mul(
                        bufD[64:128, _hp * 512:(_hp + 1) * 512], bufA[0:64, _slB:_slB + 512],
                        rcb[0:64, 512:1024]),
                        waits=[("v", vrB)] + (P.bw("bufDmv") if _hp == 0 else []), inc="v")
                    P.bset(f"nsl{_hp % 2}B", ("v", evB))
                    P.bset("rcbB", ("v", evB))
                return _emit_div
            pending_ln.append(_emit_ln)
        while pending_ln:
            pending_div.append(pending_ln.pop(0)())
        while pending_div:
            pending_div.pop(0)()
        kt_free_t = P.cnt["t"]
        kt_free_s = P.cnt["s"]
        P.bset("kt", ("t", kt_free_t), ("s", kt_free_s), ("v", P.cnt["v"]))
        if it + 1 < NT:
            emit_lq(it + 1)
        P.bset("eqt", ("t", P.cnt["t"]))
        P.bset("sqch", ("t", P.cnt["t"]))
        attn_v = P.cnt["v"]
        for oc in range(8):
            bank = (ps_c, ps_d, ps_e, ps_f)[oc % 4]
            bkey = "bank_" + BK[id(bank)]

            def wo_mm(t, _oc=oc, _bank=bank):
                last = None
                for kc in range(8):
                    last = t.matmul(_bank[:, :], wchunk(wo, _oc, kc), bufD[:, kc * 512:(kc + 1) * 512],
                                    start=(kc == 0), stop=(kc == 7))
                return last
            tw = P.op("tensor", wo_mm,
                      waits=(([("v", attn_v), ("lg", wom)]) if oc == 0 else []) + P.bw(bkey), inc="t")
            sw_ = P.op("scalar", lambda h, _oc=oc, _bank=bank: h.activation(
                out=t1sc[:, (_oc % 2) * 512:(_oc % 2 + 1) * 512], in_=_bank[:, :], func=AF.Identity,
                bias=bcol[:, 24 + _oc:25 + _oc], scale=1.0),
                waits=[("t", tw)] + P.bw(f"t1sc{oc % 2}"), inc="s")
            P.bset(bkey, ("s", sw_))
            va = P.op("vector", lambda v, _oc=oc: v.tensor_add(
                bufB[:, _oc * 512:(_oc + 1) * 512], bufB[:, _oc * 512:(_oc + 1) * 512],
                t1sc[:, (_oc % 2) * 512:(_oc % 2 + 1) * 512]),
                waits=[("s", sw_), ("lg", lx1)] +
                      ([("st", prev_out_store)] if (oc == 0 and prev_out_store) else []), inc="v")
            P.bset(f"t1sc{oc % 2}", ("v", va))
        x2_done = P.cnt["v"]
        vr3, _ = ln_stats(bufB, want_rinvs=True)
        bm3 = bcast_b(rows_b[0:1, 0:512], ps_b, "bank_b", wait_marks=[("v", P.cnt["v"])])
        br3 = bcast_f(rows[0:1, 1024:1536], ps_a, "bank_a", wait_marks=[("v", vr3)])
        qm3 = None
        qmark3 = {}
        for e in range(8):
            hf = e % 2
            P.op("vector", lambda v, _e=e, _hf=hf: v.scalar_tensor_tensor(
                out=t1sc[:, _hf * 512:(_hf + 1) * 512], in0=bufB[:, _e * 512:(_e + 1) * 512],
                scalar=1.0, in1=ps_b[:, :], op0=ALU.mult, op1=ALU.subtract),
                waits=[("t", bm3)] + P.bw(f"t1sc{hf}"), inc="v")
            qm3 = P.op("vector", lambda v, _e=e, _hf=hf: v.tensor_mul(
                q8t[:, _e * 512:(_e + 1) * 512], t1sc[:, _hf * 512:(_hf + 1) * 512], ps_a[:, :]),
                waits=[("t", br3)], inc="v")
            P.bset(f"t1sc{hf}", ("v", qm3))
            qmark3[e] = qm3
        P.bset("bank_a", ("v", qm3))
        P.bset("bank_b", ("v", qm3))
        for oc in range(8):
            bank = (ps_c, ps_d, ps_e, ps_f)[oc % 4]
            bkey = "bank_" + BK[id(bank)]
            tm = None
            for kp in range(4):
                tm = P.op("tensor", lambda t, _oc=oc, _kp=kp, _bank=bank: t.matmul(
                    _bank[:, :],
                    wchunk(w8, _oc, 2 * _kp, width=256).rearrange("p (k c) -> p k c", k=2),
                    q8t[:, _kp * 1024:(_kp + 1) * 1024].rearrange("p (k n) -> p k n", k=2),
                    start=(_kp == 0), stop=(_kp == 3),
                    perf_mode=mybir.MatmulPerfMode.DoubleRow),
                    waits=([("v", qmark3[2 * kp + 1])] if oc == 0 else []) +
                          (P.bw(bkey) if kp == 0 else []),
                    inc="t" if kp == 3 else None)
            sm = P.op("scalar", lambda h, _oc=oc, _bank=bank: h.activation(
                out=t1sc[:, (_oc % 2) * 512:(_oc % 2 + 1) * 512], in_=_bank[:, :], func=AF.Identity,
                bias=hbf[:, _oc:_oc + 1], scale=smallcol[:, 8:9]),
                waits=[("t", tm)] + P.bw(f"t1sc{oc % 2}"), inc="s")
            P.bset(bkey, ("s", sm))
            vo = P.op("vector", lambda v, _oc=oc: v.tensor_add(
                bufB[:, _oc * 512:(_oc + 1) * 512], bufB[:, _oc * 512:(_oc + 1) * 512],
                t1sc[:, (_oc % 2) * 512:(_oc % 2 + 1) * 512]),
                waits=[("s", sm)], inc="v")
            P.bset(f"t1sc{oc % 2}", ("v", vo))
        out_done = P.cnt["v"]
        P.bset("bufDc", ("v", out_done))
        so = P.op("sync", lambda s, _it=it: s.dma_start(
            out=xoT.rearrange("(e p) t -> p e t", p=128)[:, :, _it * TS:(_it + 1) * TS],
            in_=bufB[:].rearrange("p (e t) -> p e t", e=8)),
            waits=[("v", out_done)], inc="st")
        prev_out_store = so
        if it + 1 < NT:
            emit_lx1(it + 1, [("st", so), ("v", out_done)])
        P.bset("eps16", ("t", P.cnt["t"]))
        P.bset("bufD", ("t", P.cnt["t"]))
        P.bset("bufDmv", ("t", P.cnt["t"]))
    final_st = P.cnt["st"]

    with stack:
        with (
            nc.semaphore("ld") as s_ld,
            nc.semaphore("lg") as s_lg,
            nc.semaphore("st") as s_st,
            nc.semaphore("t") as s_t,
            nc.semaphore("v") as s_v,
            nc.semaphore("s") as s_s,
            nc.semaphore("g") as s_g,
            nc.semaphore("cc") as s_cc,
            nc.Block() as block,
        ):
            P.sem = {"ld": s_ld, "lg": s_lg, "st": s_st, "t": s_t, "v": s_v,
                     "s": s_s, "g": s_g, "cc": s_cc}

            @block.sync
            def _(h):
                for fn in P.ops["sync"]:
                    fn(h)
                h.wait_ge(s_st, final_st)

            @block.tensor
            def _(h):
                for fn in P.ops["tensor"]:
                    fn(h)

            @block.vector
            def _(h):
                for fn in P.ops["vector"]:
                    fn(h)

            @block.scalar
            def _(h):
                for fn in P.ops["scalar"]:
                    fn(h)

            @block.gpsimd
            def _(h):
                for fn in P.ops["gpsimd"]:
                    fn(h)
    return nc


def _prep_weight(w):
    return np.ascontiguousarray(
        w.reshape(8, 128, 8, 128).transpose(3, 0, 2, 1).reshape(128, 8192))


def _host_inputs(L0_attn_x, Wq, bq, Wk, bk, Wv, bv, Wo, bo, proj,
                 w8, bf_, iscale, wscale):
    base = {}
    base["wq_h"] = _prep_weight(np.asarray(Wq, np.float32))
    base["wk_h"] = _prep_weight(np.asarray(Wk, np.float32))
    base["wv_h"] = _prep_weight(np.asarray(Wv, np.float32))
    base["wo_h"] = _prep_weight(np.asarray(Wo, np.float32))
    base["w8_h"] = _prep_weight(np.asarray(w8, np.float32))
    base["proj_h"] = np.ascontiguousarray(
        np.concatenate([np.asarray(proj, np.float32).T] * 2, axis=0))
    bc = np.zeros((128, 40), np.float32)
    for i, b_ in enumerate([bq, bk, bv, bo, bf_]):
        bc[:, i * 8:(i + 1) * 8] = np.asarray(b_, np.float32).reshape(8, 128).T
    base["bcol_h"] = bc
    base["bvb_h"] = np.ascontiguousarray(
        np.broadcast_to(np.asarray(bv, np.float32)[None, :], (128, 1024)))
    sc = np.ones((1, 8), np.float32)
    sc[0, 0] = float(np.asarray(iscale).reshape(-1)[0])
    sc[0, 1] = float(np.asarray(wscale).reshape(-1)[0])
    base["sc_h"] = sc
    s16 = np.zeros((128, 128), np.float32)
    for c in range(8):
        s16[0:64, c * 16 + 2 * c] = NDIAG
        s16[64:128, c * 16 + 2 * c + 1] = NDIAG
    base["sel16_h"] = s16.astype(ml_dtypes.bfloat16)
    sh = np.zeros((128, 256), np.float32)
    for h in range(16):
        sh[:, h * 16 + h] = 1.0
    base["selh_h"] = sh.astype(ml_dtypes.bfloat16)
    base["ones_f_h"] = np.ones((128, 128), np.float32)
    base["ones_b_h"] = np.ones((128, 128), np.float32).astype(ml_dtypes.bfloat16)
    base["idn_f_h"] = np.eye(128, dtype=np.float32)
    ckv = np.zeros((128, 4), np.float32)
    ckv[:, 0] = -K0
    base["ck_h"] = ckv
    mk = np.zeros((16, 1040), np.float32)
    for h in range(16):
        mk[h, h * 65:(h + 1) * 65] = EPS
    base["maskt_h"] = mk.astype(ml_dtypes.bfloat16)
    return base


def kernel(L0_attn_x, L0_attn_Wq, L0_attn_bq, L0_attn_Wk, L0_attn_bk,
           L0_attn_Wv, L0_attn_bv, L0_attn_Wo, L0_attn_bo, L0_attn_proj,
           L0_ffn_weight_fp8, L0_ffn_bias, L0_ffn_input_scale, L0_ffn_weight_scale):
    x = np.asarray(L0_attn_x, np.float32)
    base = _host_inputs(x, L0_attn_Wq, L0_attn_bq, L0_attn_Wk, L0_attn_bk,
                        L0_attn_Wv, L0_attn_bv, L0_attn_Wo, L0_attn_bo,
                        L0_attn_proj, L0_ffn_weight_fp8, L0_ffn_bias,
                        L0_ffn_input_scale, L0_ffn_weight_scale)
    nc = build_program()
    in_maps = []
    for c in range(8):
        b_, tc = c // 4, c % 4
        xs = np.ascontiguousarray(x[b_, tc * T:(tc + 1) * T, :].T)
        m = dict(base)
        m["xT"] = xs
        in_maps.append(m)
    res = run_bass_kernel_spmd(nc, in_maps, core_ids=list(range(8)))
    out = np.empty((B, N, E), np.float32)
    for c in range(8):
        b_, tc = c // 4, c % 4
        out[b_, tc * T:(tc + 1) * T, :] = res.results[c]["xoT"].T
    return out


if __name__ == "__main__":
    z = np.load("/root/problem/exp/ref_data.npz")
    ins = {k: z[k] for k in z.files if k != "expected"}
    got = kernel(**ins)
    exp = z["expected"]
    d = got - exp
    rvr = float((d ** 2).mean() / (exp ** 2).mean())
    print(f"kernel vs reference: rvr={rvr:.3e} relL2={np.sqrt(rvr):.3e}")
